# version 07
# baseline (speedup 1.0000x reference)
import contextlib
import math

import numpy as np

import concourse.bass as bass
import concourse.mybir as mybir
from concourse.bass_utils import run_bass_kernel_spmd

F32 = mybir.dt.float32
BF16 = mybir.dt.bfloat16
AF = mybir.ActivationFunctionType
ALU = mybir.AluOpType
AX = mybir.AxisListType

B, T, D, H = 2, 8192, 1024, 16
HD = D // H
NP = H // 2
SCALE = 1.0 / math.sqrt(HD)
TOK = B * T
OWN = TOK // 8
NC_D = D // 128
BLK = 512
NBLK_B = T // BLK
NBLK_O = OWN // BLK
EPS = 1e-5
NKC = T // 128
QGRP = 512
NQT = QGRP // 128
NQG = OWN // QGRP

FFN_KEYS = ["cn_w1", "cn_b1", "cn_w2", "cn_b2", "cn_w3", "cn_b3", "cn_w4",
            "cn_b4", "gc_w1", "gc_b1", "gc_w2", "gc_b2", "gc_w3", "gc_b3", "ar"]


class Ctr:
    def __init__(self, sem):
        self.sem = sem
        self.v = 0

    def inc(self, n=1):
        self.v += n
        return self.v


def _mk_sems(nc, es):
    return {k: Ctr(es.enter_context(nc.semaphore(f"sem_{k}")))
            for k in ("t", "s", "v", "d", "g", "c", "x0", "x1", "p0", "p1", "k0", "k1")}


def _declare_ffn_hbm(nc):
    return {
        "cn_w1": nc.dram_tensor("cn_w1", [D, D], F32, kind="ExternalInput").ap(),
        "cn_b1": nc.dram_tensor("cn_b1", [D], F32, kind="ExternalInput").ap(),
        "cn_w2": nc.dram_tensor("cn_w2", [D, 512], F32, kind="ExternalInput").ap(),
        "cn_b2": nc.dram_tensor("cn_b2", [512], F32, kind="ExternalInput").ap(),
        "cn_w3": nc.dram_tensor("cn_w3", [512, 256], F32, kind="ExternalInput").ap(),
        "cn_b3": nc.dram_tensor("cn_b3", [256], F32, kind="ExternalInput").ap(),
        "cn_w4": nc.dram_tensor("cn_w4", [256, 1], F32, kind="ExternalInput").ap(),
        "cn_b4": nc.dram_tensor("cn_b4", [1], F32, kind="ExternalInput").ap(),
        "gc_w1": nc.dram_tensor("gc_w1", [D, 512], F32, kind="ExternalInput").ap(),
        "gc_b1": nc.dram_tensor("gc_b1", [512], F32, kind="ExternalInput").ap(),
        "gc_w2": nc.dram_tensor("gc_w2", [512, 256], F32, kind="ExternalInput").ap(),
        "gc_b2": nc.dram_tensor("gc_b2", [256], F32, kind="ExternalInput").ap(),
        "gc_w3": nc.dram_tensor("gc_w3", [256, 1], F32, kind="ExternalInput").ap(),
        "gc_b3": nc.dram_tensor("gc_b3", [1], F32, kind="ExternalInput").ap(),
        "ar": nc.dram_tensor("ar", [1], F32, kind="ExternalInput").ap(),
    }


def _load_consts(nc, block, es, sems):
    ones128 = nc.inline_tensor(np.ones((128, 128), dtype=np.float32), "ones128")
    onesrow = nc.inline_tensor(np.ones((1, 128), dtype=np.float32), "onesrow")
    iden = nc.inline_tensor(np.eye(128, dtype=np.float32), "iden128")
    ab = np.zeros((128, 2), dtype=np.float32)
    ab[0:64, 0] = 1.0
    ab[64:128, 1] = 1.0
    onesab = nc.inline_tensor(ab, "onesAB")
    rsela = np.zeros((1, 128), dtype=np.float32)
    rsela[0, 0:64] = 1.0
    rselb = np.zeros((1, 128), dtype=np.float32)
    rselb[0, 64:128] = 1.0
    rsa = nc.inline_tensor(rsela, "rsela")
    rsb = nc.inline_tensor(rselb, "rselb")
    h2sel = np.zeros((2, 2, 128), dtype=np.float32)
    h2sel[0, 0, :] = 1.0
    h2sel[1, 1, :] = 1.0
    hselm = np.zeros((128, 16), dtype=np.float32)
    for _h in range(16):
        hselm[_h * 8:(_h + 1) * 8, _h] = 1.0
    rsabm = np.zeros((2, 128), dtype=np.float32)
    rsabm[0, 0:64] = 1.0
    rsabm[1, 64:128] = 1.0
    names = {
        "ones128_sb": es.enter_context(nc.sbuf_tensor("ones128_sb", [128, 128], F32)),
        "onesrow_sb": es.enter_context(nc.sbuf_tensor("onesrow_sb", [1, 128], F32)),
        "iden_sb": es.enter_context(nc.sbuf_tensor("iden_sb", [128, 128], F32)),
        "onesab_sb": es.enter_context(nc.sbuf_tensor("onesab_sb", [128, 2], F32)),
        "rsa_sb": es.enter_context(nc.sbuf_tensor("rsa_sb", [1, 128], F32)),
        "rsb_sb": es.enter_context(nc.sbuf_tensor("rsb_sb", [1, 128], F32)),
        "h2sel_sb": es.enter_context(nc.sbuf_tensor("h2sel_sb", [2, 2, 128], F32)),
        "hsel_sb": es.enter_context(nc.sbuf_tensor("hsel_sb", [128, 16], F32)),
        "rsab_sb": es.enter_context(nc.sbuf_tensor("rsab_sb", [2, 128], F32)),
    }
    c_c = sems["c"]

    @block.sync
    def _(s):
        for sb, src in (("ones128_sb", ones128), ("onesrow_sb", onesrow),
                        ("iden_sb", iden), ("onesab_sb", onesab), ("rsa_sb", rsa),
                        ("rsb_sb", rsb), ("h2sel_sb", nc.inline_tensor(h2sel, "h2sel")),
                        ("hsel_sb", nc.inline_tensor(hselm, "hselm")),
                        ("rsab_sb", nc.inline_tensor(rsabm, "rsabm"))):
            s.dma_start(out=names[sb][...], in_=src.ap()[...]).then_inc(c_c.sem, 16)
    c_c.inc(9 * 16)
    cv = c_c.v

    @block.tensor
    def _(t):
        t.wait_ge(c_c.sem, cv)

    @block.scalar
    def _(s):
        s.wait_ge(c_c.sem, cv)

    @block.vector
    def _(v):
        v.wait_ge(c_c.sem, cv)

    @block.gpsimd
    def _(g):
        g.wait_ge(c_c.sem, cv)
    return names


def _emit_stats(nc, block, es, names, x_sb, st, sems, gate_sem, gate_val):
    t_c, s_c, v_c = sems["t"], sems["s"], sems["v"]
    ones128 = names["ones128_sb"]
    ps_sum, ps_sq, xsq, a_sb, b_sb = (st["ps_sum"], st["ps_sq"], st["xsq"],
                                      st["a_sb"], st["b_sb"])
    prev_mm = st["prev"].get("mm", 0)
    prev_var = st["prev"].get("var", 0)

    @block.scalar
    def _(s):
        s.wait_ge(gate_sem, gate_val)
        s.wait_ge(t_c.sem, prev_mm)
        for c in range(NC_D):
            _last = s.activation(out=xsq[:, c, :], in_=x_sb[:, c, :], func=AF.Square)
        _last.then_inc(s_c.sem, 1)
    sq_done = s_c.inc()

    @block.tensor
    def _(t):
        t.wait_ge(gate_sem, gate_val)
        t.wait_ge(v_c.sem, prev_var)
        for c in range(NC_D):
            t.matmul(ps_sum[:, :], ones128[:, :], x_sb[:, c, :],
                     start=(c == 0), stop=(c == NC_D - 1))
        t.wait_ge(s_c.sem, sq_done)
        for c in range(NC_D):
            _last = t.matmul(ps_sq[:, :], ones128[:, :], xsq[:, c, :],
                     start=(c == 0), stop=(c == NC_D - 1))
        _last.then_inc(t_c.sem, 1)
    mm_done = t_c.inc()

    @block.vector
    def _(v):
        v.wait_ge(t_c.sem, mm_done)
        v.tensor_scalar(out=b_sb[:, :], in0=ps_sum[:, :], scalar1=1.0 / D,
                        scalar2=None, op0=ALU.mult)
        v.drain()
        v.tensor_mul(out=a_sb[:, :], in0=b_sb[:, :], in1=b_sb[:, :])
        v.drain()
        _last = v.scalar_tensor_tensor(out=a_sb[:, :], in0=ps_sq[:, :], scalar=1.0 / D,
                               in1=a_sb[:, :], op0=ALU.mult, op1=ALU.subtract)
        _last.then_inc(v_c.sem, 1)
    var_done = v_c.inc()

    @block.scalar
    def _(s):
        s.wait_ge(v_c.sem, var_done)
        s.activation(out=a_sb[:, :], in_=a_sb[:, :], func=AF.Ln,
                     bias=st["eps"][:, 0:1])
        s.drain()
        s.activation(out=a_sb[:, :], in_=a_sb[:, :], func=AF.Exp, scale=-0.5).then_inc(s_c.sem, 1)
    rs_done = s_c.inc()

    @block.vector
    def _(v):
        v.wait_ge(s_c.sem, rs_done)
        _last = v.scalar_tensor_tensor(out=b_sb[:, :], in0=b_sb[:, :], scalar=-1.0,
                               in1=a_sb[:, :], op0=ALU.mult, op1=ALU.mult)
        _last.then_inc(v_c.sem, 1)
    ab_ready = v_c.inc()
    st["prev"] = {"mm": mm_done, "var": var_done}
    return ab_ready


def _emit_ln_apply(block, x_sb, ln_sb, a_sb, b_sb, sems, ab_ready):
    v_c = sems["v"]

    @block.vector
    def _(v):
        v.wait_ge(v_c.sem, ab_ready)
        for c in range(NC_D):
            v.tensor_mul(out=ln_sb[:, c, :], in0=x_sb[:, c, :], in1=a_sb[:, :])
        v.drain()
        for c in range(NC_D):
            _last = v.tensor_add(out=ln_sb[:, c, :], in0=ln_sb[:, c, :], in1=b_sb[:, :])
        _last.then_inc(v_c.sem, 1)
    return v_c.inc()


def _ffn_weight_loads(nc, block, es, hbm, sems, fence=None):
    d_c = sems["d"]
    w1 = es.enter_context(nc.sbuf_tensor("w1_sb", [128, NC_D, D], F32))
    w2 = es.enter_context(nc.sbuf_tensor("w2_sb", [128, NC_D, 512], F32))
    w3 = es.enter_context(nc.sbuf_tensor("w3_sb", [128, 4, 256], F32))
    w4 = es.enter_context(nc.sbuf_tensor("w4_sb", [128, 2, 1], F32))
    b1c = es.enter_context(nc.sbuf_tensor("b1_sb", [128, NC_D], F32))
    b2c = es.enter_context(nc.sbuf_tensor("b2_sb", [128, 4], F32))
    b3c = es.enter_context(nc.sbuf_tensor("b3_sb", [128, 2], F32))
    b4c = es.enter_context(nc.sbuf_tensor("b4_sb", [1, 1], F32))
    g1 = es.enter_context(nc.sbuf_tensor("g1_sb", [128, NC_D, 512], F32))
    g2 = es.enter_context(nc.sbuf_tensor("g2_sb", [128, 4, 256], F32))
    g3 = es.enter_context(nc.sbuf_tensor("g3_sb", [128, 2, 1], F32))
    gb1 = es.enter_context(nc.sbuf_tensor("gb1_sb", [128, 4], F32))
    gb2 = es.enter_context(nc.sbuf_tensor("gb2_sb", [128, 2], F32))
    gb3 = es.enter_context(nc.sbuf_tensor("gb3_sb", [1, 1], F32))
    arsb = es.enter_context(nc.sbuf_tensor("ar_sb", [1, 1], F32))

    @block.sync
    def _(s):
        if fence is not None:
            s.wait_ge(sems["t"].sem, fence[0])
            s.wait_ge(sems["s"].sem, fence[1])
            s.wait_ge(sems["v"].sem, fence[2])
            s.wait_ge(sems["g"].sem, fence[3])
        for kc in range(NC_D):
            s.dma_start(out=w1[:, kc, :], in_=hbm["cn_w1"][kc * 128:(kc + 1) * 128, :]).then_inc(d_c.sem, 16)
            s.dma_start(out=w2[:, kc, :], in_=hbm["cn_w2"][kc * 128:(kc + 1) * 128, :]).then_inc(d_c.sem, 16)
            s.dma_start(out=g1[:, kc, :], in_=hbm["gc_w1"][kc * 128:(kc + 1) * 128, :]).then_inc(d_c.sem, 16)
        for kc in range(4):
            s.dma_start(out=w3[:, kc, :], in_=hbm["cn_w3"][kc * 128:(kc + 1) * 128, :]).then_inc(d_c.sem, 16)
            s.dma_start(out=g2[:, kc, :], in_=hbm["gc_w2"][kc * 128:(kc + 1) * 128, :]).then_inc(d_c.sem, 16)
        for kc in range(2):
            s.dma_start(out=w4[:, kc, :], in_=hbm["cn_w4"][kc * 128:(kc + 1) * 128, :]).then_inc(d_c.sem, 16)
            s.dma_start(out=g3[:, kc, :], in_=hbm["gc_w3"][kc * 128:(kc + 1) * 128, :]).then_inc(d_c.sem, 16)
        s.dma_start(out=b1c[:, :], in_=hbm["cn_b1"].rearrange("(c p) -> p c", p=128), allow_slow_non_contiguous=True).then_inc(d_c.sem, 16)
        s.dma_start(out=b2c[:, :], in_=hbm["cn_b2"].rearrange("(c p) -> p c", p=128), allow_slow_non_contiguous=True).then_inc(d_c.sem, 16)
        s.dma_start(out=b3c[:, :], in_=hbm["cn_b3"].rearrange("(c p) -> p c", p=128), allow_slow_non_contiguous=True).then_inc(d_c.sem, 16)
        s.dma_start(out=b4c[0:1, 0:1], in_=hbm["cn_b4"][None, :]).then_inc(d_c.sem, 16)
        s.dma_start(out=gb1[:, :], in_=hbm["gc_b1"].rearrange("(c p) -> p c", p=128), allow_slow_non_contiguous=True).then_inc(d_c.sem, 16)
        s.dma_start(out=gb2[:, :], in_=hbm["gc_b2"].rearrange("(c p) -> p c", p=128), allow_slow_non_contiguous=True).then_inc(d_c.sem, 16)
        s.dma_start(out=gb3[0:1, 0:1], in_=hbm["gc_b3"][None, :]).then_inc(d_c.sem, 16)
        s.dma_start(out=arsb[0:1, 0:1], in_=hbm["ar"][None, :]).then_inc(d_c.sem, 16)
    d_c.inc(16 * (3 * NC_D + 2 * 4 + 2 * 2 + 8))
    return ((w1, w2, w3, w4), (b1c, b2c, b3c, b4c), (g1, g2, g3),
            (gb1, gb2, gb3), arsb, d_c.v)


def _emit_gcnet(nc, block, es, names, gvec, gw_sb, gb_sb, gf_out, psg, sems,
                g_ready, denom_tokens, tag, wload=0):
    t_c, s_c, v_c = sems["t"], sems["s"], sems["v"]
    g1, g2, g3 = gw_sb
    gb1, gb2, gb3 = gb_sb
    h1 = es.enter_context(nc.sbuf_tensor(f"gch1{tag}", [128, 4], F32))
    h2 = es.enter_context(nc.sbuf_tensor(f"gch2{tag}", [128, 2], F32))
    gmean = es.enter_context(nc.sbuf_tensor(f"gcm{tag}", [128, NC_D], F32))

    @block.vector
    def _(v):
        v.wait_ge(v_c.sem, g_ready)
        _last = v.tensor_scalar(out=gmean[:, :], in0=gvec[:, :], scalar1=1.0 / denom_tokens,
                        scalar2=None, op0=ALU.mult)
        _last.then_inc(v_c.sem, 1)
    gm_done = v_c.inc()

    @block.tensor
    def _(t):
        t.wait_ge(v_c.sem, gm_done)
        t.wait_ge(sems["d"].sem, wload)
        for mc in range(4):
            for kc in range(NC_D):
                _last = t.matmul(psg[:, mc:mc + 1], g1[:, kc, mc * 128:(mc + 1) * 128],
                         gmean[:, kc:kc + 1], start=(kc == 0), stop=(kc == NC_D - 1))
        _last.then_inc(t_c.sem, 1)
    mm1 = t_c.inc()

    @block.scalar
    def _(s):
        s.wait_ge(t_c.sem, mm1)
        for mc in range(4):
            _last = s.activation(out=h1[:, mc:mc + 1], in_=psg[:, mc:mc + 1], func=AF.Gelu,
                         bias=gb1[:, mc:mc + 1])
        _last.then_inc(s_c.sem, 1)
    a1 = s_c.inc()

    @block.tensor
    def _(t):
        t.wait_ge(s_c.sem, a1)
        for mc in range(2):
            for kc in range(4):
                _last = t.matmul(psg[:, mc:mc + 1], g2[:, kc, mc * 128:(mc + 1) * 128],
                         h1[:, kc:kc + 1], start=(kc == 0), stop=(kc == 3))
        _last.then_inc(t_c.sem, 1)
    mm2 = t_c.inc()

    @block.scalar
    def _(s):
        s.wait_ge(t_c.sem, mm2)
        for mc in range(2):
            _last = s.activation(out=h2[:, mc:mc + 1], in_=psg[:, mc:mc + 1], func=AF.Gelu,
                         bias=gb2[:, mc:mc + 1])
        _last.then_inc(s_c.sem, 1)
    a2 = s_c.inc()

    @block.tensor
    def _(t):
        t.wait_ge(s_c.sem, a2)
        for kc in range(2):
            _last = t.matmul(psg[0:1, 3:4], g3[:, kc, 0:1], h2[:, kc:kc + 1],
                     start=(kc == 0), stop=(kc == 1))
        _last.then_inc(t_c.sem, 1)
    mm3 = t_c.inc()

    @block.scalar
    def _(s):
        s.wait_ge(t_c.sem, mm3)
        _last = s.activation(out=gf_out[0:1, 0:1], in_=psg[0:1, 3:4], func=AF.Sigmoid,
                     bias=gb3[0:1, 0:1])
        _last.then_inc(s_c.sem, 1)
    return s_c.inc()


def _emit_ab_scalars(block, gf, arsb, AB_sb, sems, gf_done, wload):
    s_c, v_c, d_c = sems["s"], sems["v"], sems["d"]

    @block.vector
    def _(v):
        v.wait_ge(s_c.sem, gf_done)
        v.wait_ge(d_c.sem, wload)
        v.tensor_scalar(out=AB_sb[0:1, 0:1], in0=gf[0:1, 0:1], scalar1=0.7,
                        scalar2=-0.5, op0=ALU.mult, op1=ALU.add)
        v.drain()
        v.tensor_mul(out=AB_sb[0:1, 0:1], in0=AB_sb[0:1, 0:1], in1=arsb[0:1, 0:1])
        v.drain()
        v.tensor_scalar(out=AB_sb[0:1, 0:1], in0=AB_sb[0:1, 0:1], scalar1=2.0,
                        scalar2=0.1, op0=ALU.mult, op1=ALU.add)
        v.tensor_scalar(out=AB_sb[0:1, 1:2], in0=arsb[0:1, 0:1], scalar1=0.6,
                        scalar2=None, op0=ALU.mult)
        v.drain()
        v.drain().then_inc(v_c.sem, 1)
    return v_c.inc()


def _emit_ffn_cycles(nc, block, names, w_sb, b_sb, bufs, h1, h2, h3,
                     ps_mm, ps_bc, tf_row, c_row, AB_sb, sems, in_ready, wload=0):
    t_c, s_c, v_c = sems["t"], sems["s"], sems["v"]
    onesrow = names["onesrow_sb"]
    w1, w2, w3, w4 = w_sb
    b1c, b2c, b3c, b4c = b_sb
    wv = in_ready
    for cyc in range(3):
        cur = bufs[cyc % 2]
        nxt = bufs[(cyc + 1) % 2]

        for (wt, bt, hsrc, hdst, n_m, n_k) in (
                (w1, b1c, cur, h1, NC_D, NC_D),
                (w2, b2c, h1, h2, 4, NC_D),
                (w3, b3c, h2, h3, 2, 4)):
            act_base = s_c.v
            t_base = t_c.v

            @block.tensor
            def _(t, wt=wt, hsrc=hsrc, n_m=n_m, n_k=n_k, wv=wv,
                  act_base=act_base, first=(hsrc is cur), cyc=cyc):
                if first:
                    t.wait_ge(v_c.sem, wv)
                    if cyc == 0:
                        t.wait_ge(sems["d"].sem, wload)
                else:
                    t.wait_ge(s_c.sem, act_base)
                for mc in range(n_m):
                    if mc >= 2:
                        t.wait_ge(s_c.sem, act_base + mc - 1)
                    for kc in range(n_k):
                        _last = t.matmul(ps_mm[:, mc % 2, :], wt[:, kc, mc * 128:(mc + 1) * 128],
                                 hsrc[:, kc, :], start=(kc == 0), stop=(kc == n_k - 1))
                    _last.then_inc(t_c.sem, 1)
            t_c.inc(n_m)

            @block.scalar
            def _(s, bt=bt, hdst=hdst, n_m=n_m, t_base=t_base):
                for mc in range(n_m):
                    s.wait_ge(t_c.sem, t_base + mc + 1)
                    s.activation(out=hdst[:, mc, :], in_=ps_mm[:, mc % 2, :],
                                 func=AF.Gelu, bias=bt[:, mc:mc + 1]
                                 ).then_inc(s_c.sem, 1)
            s_c.inc(n_m)
        h3_done = s_c.v

        @block.tensor
        def _(t, h3_done=h3_done):
            t.wait_ge(s_c.sem, h3_done)
            for kc in range(2):
                _last = t.matmul(ps_mm[0:1, 0, :], w4[:, kc, 0:1], h3[:, kc, :],
                         start=(kc == 0), stop=(kc == 1))
            _last.then_inc(t_c.sem, 1)
        z_done = t_c.inc()

        @block.scalar
        def _(s, z_done=z_done):
            s.wait_ge(t_c.sem, z_done)
            s.activation(out=tf_row[0:1, :], in_=ps_mm[0:1, 0, :], func=AF.Sigmoid,
                         bias=b4c[0:1, 0:1]).then_inc(s_c.sem, 1)
        tf_done = s_c.inc()

        @block.vector
        def _(v, tf_done=tf_done):
            v.wait_ge(s_c.sem, tf_done)
            _last = v.tensor_scalar(out=c_row[0:1, :], in0=tf_row[0:1, :],
                            scalar1=AB_sb[0:1, 1:2], scalar2=AB_sb[0:1, 0:1],
                            op0=ALU.mult, op1=ALU.add)
            _last.then_inc(v_c.sem, 1)
        c_done = v_c.inc()

        @block.tensor
        def _(t, c_done=c_done):
            t.wait_ge(v_c.sem, c_done)
            _last = t.matmul(ps_bc[:, :], onesrow[0:1, :], c_row[0:1, :],
                     start=True, stop=True)
            _last.then_inc(t_c.sem, 1)
        bc_done = t_c.inc()

        @block.vector
        def _(v, cur=cur, nxt=nxt, bc_done=bc_done):
            v.wait_ge(t_c.sem, bc_done)
            sig = [(4, 1.0), (5, 1.0), (6, -1.0), (7, -1.0),
                   (0, -1.0), (1, -1.0), (2, 1.0), (3, 1.0)]
            for c in range(NC_D):
                srcc, sgn = sig[c]
                if sgn > 0:
                    v.tensor_mul(out=nxt[:, c, :], in0=cur[:, srcc, :], in1=ps_bc[:, :])
                else:
                    v.scalar_tensor_tensor(out=nxt[:, c, :], in0=cur[:, srcc, :],
                                           scalar=-1.0, in1=ps_bc[:, :],
                                           op0=ALU.mult, op1=ALU.mult)
            v.drain()
            for c in range(NC_D):
                _last = v.tensor_add(out=nxt[:, c, :], in0=nxt[:, c, :], in1=cur[:, c, :])
            _last.then_inc(v_c.sem, 1)
        wv = v_c.inc()
    return bufs[1], wv


def _emit_ffn_block_sweep(nc, block, es, names, w_sb, b_sbw, AB_sb, st, sems,
                          src_ap, out_ap, half_coef, src_gate, tag, fence=None,
                          wload=0):
    t_c, s_c, v_c, d_c, g_c = (sems[k] for k in ("t", "s", "v", "d", "g"))
    xblk = es.enter_context(nc.sbuf_tensor(f"fxb{tag}", [128, 2, NC_D, BLK], F32))
    outa = es.enter_context(nc.sbuf_tensor(f"fa{tag}", [128, NC_D, BLK], F32))
    outb = es.enter_context(nc.sbuf_tensor(f"fb{tag}", [128, NC_D, BLK], F32))
    h1 = es.enter_context(nc.sbuf_tensor(f"fh1{tag}", [128, NC_D, BLK], F32))
    h2 = es.enter_context(nc.sbuf_tensor(f"fh2{tag}", [128, 4, BLK], F32))
    h3 = es.enter_context(nc.sbuf_tensor(f"fh3{tag}", [128, 2, BLK], F32))
    tf_row = es.enter_context(nc.sbuf_tensor(f"ftf{tag}", [1, BLK], F32))
    c_row = es.enter_context(nc.sbuf_tensor(f"fc{tag}", [1, BLK], F32))
    ps_mm = es.enter_context(nc.psum_tensor(f"fpsm{tag}", [128, 2, BLK], F32))
    ps_bc = es.enter_context(nc.psum_tensor(f"fpsb{tag}", [128, BLK], F32))
    ofree = {}
    ystore = {}
    for ob in range(NBLK_O):
        buf = ob % 2

        x_c = sems[f"x{buf}"]

        @block.sync
        def _(s, ob=ob, buf=buf):
            if ob == 0 and fence is not None:
                s.wait_ge(t_c.sem, fence[0])
                s.wait_ge(s_c.sem, fence[1])
                s.wait_ge(v_c.sem, fence[2])
            if src_gate is not None:
                s.wait_ge(src_gate[0], src_gate[1])
            if ob >= 2:
                s.wait_ge(v_c.sem, ofree[ob - 2])
            for c in range(NC_D):
                s.dma_start(out=xblk[:, buf, c, :],
                            in_=src_ap[c * 128:(c + 1) * 128, ob * BLK:(ob + 1) * BLK]
                            ).then_inc(x_c.sem, 16)
        x_c.inc(16 * NC_D)

        ab_ready = _emit_stats(nc, block, es, names, xblk[:, buf], st, sems,
                               x_c.sem, x_c.v)
        if ob >= 1:
            @block.vector
            def _(v, ob=ob):
                v.wait_ge(g_c.sem, ystore[ob - 1])
        ln_done = _emit_ln_apply(block, xblk[:, buf], outa, st["a_sb"], st["b_sb"],
                                 sems, ab_ready)

        final_buf, fin_wv = _emit_ffn_cycles(
            nc, block, names, w_sb, b_sbw, (outa, outb), h1, h2, h3,
            ps_mm, ps_bc, tf_row, c_row, AB_sb, sems, ln_done, wload=wload)
        other_buf = outa if final_buf is outb else outb

        @block.vector
        def _(v, buf=buf, final_buf=final_buf, other_buf=other_buf, fin_wv=fin_wv):
            v.wait_ge(v_c.sem, fin_wv)
            for c in range(NC_D):
                _last = v.scalar_tensor_tensor(out=other_buf[:, c, :], in0=final_buf[:, c, :],
                                       scalar=half_coef, in1=xblk[:, buf, c, :],
                                       op0=ALU.mult, op1=ALU.add)
            _last.then_inc(v_c.sem, 1)
        y_done = v_c.inc()
        ofree[ob] = y_done

        @block.gpsimd
        def _(g, ob=ob, other_buf=other_buf, y_done=y_done):
            g.wait_ge(v_c.sem, y_done)
            for c in range(NC_D):
                g.dma_start(out=out_ap[c * 128:(c + 1) * 128, ob * BLK:(ob + 1) * BLK],
                            in_=other_buf[:, c, :]).then_inc(g_c.sem, 16)
        g_c.inc(16 * NC_D)
        ystore[ob] = g_c.v
    return g_c.v



def build_l1():
    nc = bass.Bass("TRN2", target_bir_lowering=False, debug=False)
    x_all = nc.dram_tensor("x_all", [D, T], F32, kind="ExternalInput").ap()
    x_own = nc.dram_tensor("x_own", [D, OWN], F32, kind="ExternalInput").ap()
    hbm = _declare_ffn_hbm(nc)
    x1_out = nc.dram_tensor("x1_own", [D, OWN], F32, kind="ExternalOutput").ap()

    es = contextlib.ExitStack()
    with es:
        sems = _mk_sems(nc, es)
        t_c, s_c, v_c, d_c, g_c = (sems[k] for k in ("t", "s", "v", "d", "g"))
        block = es.enter_context(nc.Block())
        names = _load_consts(nc, block, es, sems)
        (w_sb, b_sbw, gw_sb, gb_sb, arsb, wload) = _ffn_weight_loads(
            nc, block, es, hbm, sems)

        st = {
            "ps_sum": es.enter_context(nc.psum_tensor("ps_sum", [128, BLK], F32)),
            "ps_sq": es.enter_context(nc.psum_tensor("ps_sq", [128, BLK], F32)),
            "xsq": es.enter_context(nc.sbuf_tensor("xsq", [128, NC_D, BLK], F32)),
            "a_sb": es.enter_context(nc.sbuf_tensor("a_sb", [128, BLK], F32)),
            "b_sb": es.enter_context(nc.sbuf_tensor("b_sb", [128, BLK], F32)),
            "eps": es.enter_context(nc.sbuf_tensor("eps_col", [128, 1], F32)),
            "prev": {},
        }
        psg = es.enter_context(nc.psum_tensor("ps_gc", [128, 4], F32))
        gacc = es.enter_context(nc.sbuf_tensor("gacc", [128, NC_D], F32))
        rtmp = es.enter_context(nc.sbuf_tensor("rtmp", [128, NC_D], F32))
        gf = es.enter_context(nc.sbuf_tensor("gf", [1, 1], F32))
        AB_sb = es.enter_context(nc.sbuf_tensor("AB", [1, 2], F32))

        @block.vector
        def _(v):
            v.memset(gacc[:, :], 0.0)
            v.memset(st["eps"][:, :], EPS).then_inc(v_c.sem, 1)
        v_c.inc()

        with contextlib.ExitStack() as es2:
            xblk = es2.enter_context(nc.sbuf_tensor("sxb", [128, 2, NC_D, BLK], F32))
            lnt = es2.enter_context(nc.sbuf_tensor("sln", [128, NC_D, BLK], F32))
            xfree = {}
            for tb in range(NBLK_B):
                buf = tb % 2

                x_c = sems[f"x{buf}"]

                @block.sync
                def _(s, tb=tb, buf=buf):
                    if tb >= 2:
                        s.wait_ge(v_c.sem, xfree[tb - 2])
                    for c in range(NC_D):
                        s.dma_start(out=xblk[:, buf, c, :],
                                    in_=x_all[c * 128:(c + 1) * 128, tb * BLK:(tb + 1) * BLK]
                                    ).then_inc(x_c.sem, 16)
                x_c.inc(16 * NC_D)

                ab_ready = _emit_stats(nc, block, es, names, xblk[:, buf], st,
                                       sems, x_c.sem, x_c.v)
                ln_done = _emit_ln_apply(block, xblk[:, buf], lnt, st["a_sb"],
                                         st["b_sb"], sems, ab_ready)

                @block.vector
                def _(v, ln_done=ln_done):
                    v.wait_ge(v_c.sem, ln_done)
                    for c in range(NC_D):
                        v.reduce_sum(out=rtmp[:, c:c + 1], in_=lnt[:, c, :], axis=AX.X)
                    v.drain()
                    v.tensor_add(out=gacc[:, :], in0=gacc[:, :], in1=rtmp[:, :]).then_inc(v_c.sem, 1)
                xfree[tb] = v_c.inc()

        gf_done = _emit_gcnet(nc, block, es, names, gacc, gw_sb, gb_sb, gf, psg,
                              sems, xfree[NBLK_B - 1], float(T), "1", wload=wload)
        _emit_ab_scalars(block, gf, arsb, AB_sb, sems, gf_done, wload)
        fence1 = (t_c.v, s_c.v, v_c.v)

        gfin = _emit_ffn_block_sweep(nc, block, es, names, w_sb, b_sbw, AB_sb, st,
                                     sems, x_own, x1_out, 0.5, None, "1",
                                     fence=fence1, wload=wload)

        @block.gpsimd
        def _(g):
            g.wait_ge(g_c.sem, gfin)
    return nc



def build_l2():
    nc = bass.Bass("TRN2", target_bir_lowering=False, debug=False)
    x_all = nc.dram_tensor("x_all", [D, T], F32, kind="ExternalInput").ap()
    x_own = nc.dram_tensor("x_own", [D, OWN], F32, kind="ExternalInput").ap()
    wq = nc.dram_tensor("wq", [D, D], F32, kind="ExternalInput").ap()
    wk = nc.dram_tensor("wk", [D, D], F32, kind="ExternalInput").ap()
    wv = nc.dram_tensor("wv", [D, D], F32, kind="ExternalInput").ap()
    bq = nc.dram_tensor("bq", [D], F32, kind="ExternalInput").ap()
    bk = nc.dram_tensor("bk", [D], F32, kind="ExternalInput").ap()
    bv = nc.dram_tensor("bv", [D], F32, kind="ExternalInput").ap()
    otp_out = nc.dram_tensor("otp", [NP, 128, OWN], F32, kind="ExternalOutput").ap()
    l_out = nc.dram_tensor("l_out", [H, OWN], F32, kind="ExternalOutput").ap()
    nsq_out = nc.dram_tensor("nsq_out", [H, OWN], F32, kind="ExternalOutput").ap()
    kfmh = nc.dram_tensor("kfmh_scr", [NP, 128, T], BF16, kind="Internal").ap()
    kfml = nc.dram_tensor("kfml_scr", [NP, 128, T], BF16, kind="Internal").ap()
    vtm = nc.dram_tensor("vtm_scr", [T, D], F32, kind="Internal").ap()
    qfmh = nc.dram_tensor("qfmh_scr", [NP, 128, OWN], BF16, kind="Internal").ap()
    qfml = nc.dram_tensor("qfml_scr", [NP, 128, OWN], BF16, kind="Internal").ap()

    es = contextlib.ExitStack()
    with es:
        sems = _mk_sems(nc, es)
        t_c, s_c, v_c, d_c, g_c = (sems[k] for k in ("t", "s", "v", "d", "g"))
        block = es.enter_context(nc.Block())
        names = _load_consts(nc, block, es, sems)

        kmax2 = es.enter_context(nc.sbuf_tensor("kmax2", [2, NP], F32))
        qmax2 = es.enter_context(nc.sbuf_tensor("qmax2", [2, NP], F32))
        mval = es.enter_context(nc.sbuf_tensor("mval", [2, NP], F32))
        mcol = es.enter_context(nc.sbuf_tensor("mcol", [128, 2 * NP], F32))

        @block.vector
        def _(v):
            v.memset(kmax2[:, :], 0.0)
            v.memset(qmax2[:, :], 0.0).then_inc(v_c.sem, 1)
        v_c.inc()

        with contextlib.ExitStack() as es2:
            wq_sb = es2.enter_context(nc.sbuf_tensor("wq_sb", [128, NC_D, D], F32))
            wk_sb = es2.enter_context(nc.sbuf_tensor("wk_sb", [128, NC_D, D], F32))
            wv_sb = es2.enter_context(nc.sbuf_tensor("wv_sb", [128, NC_D, D], F32))
            bq_sb = es2.enter_context(nc.sbuf_tensor("bq_sb", [128, NP], F32))
            bk_sb = es2.enter_context(nc.sbuf_tensor("bk_sb", [128, NP], F32))
            bv_sb = es2.enter_context(nc.sbuf_tensor("bv_sb", [1, D], F32))

            @block.sync
            def _(s):
                for kc in range(NC_D):
                    s.dma_start(out=wq_sb[:, kc, :], in_=wq[kc * 128:(kc + 1) * 128, :]).then_inc(d_c.sem, 16)
                    s.dma_start(out=wk_sb[:, kc, :], in_=wk[kc * 128:(kc + 1) * 128, :]).then_inc(d_c.sem, 16)
                    s.dma_start(out=wv_sb[:, kc, :], in_=wv[kc * 128:(kc + 1) * 128, :]).then_inc(d_c.sem, 16)
                s.dma_start(out=bq_sb[:, :], in_=bq.rearrange("(c p) -> p c", p=128), allow_slow_non_contiguous=True).then_inc(d_c.sem, 16)
                s.dma_start(out=bk_sb[:, :], in_=bk.rearrange("(c p) -> p c", p=128), allow_slow_non_contiguous=True).then_inc(d_c.sem, 16)
                s.dma_start(out=bv_sb[0:1, :], in_=bv[None, :]).then_inc(d_c.sem, 16)
            d_c.inc(16 * (3 * NC_D + 3))
            wload = d_c.v

            st = {
                "ps_sum": es2.enter_context(nc.psum_tensor("ps_sum", [128, BLK], F32)),
                "ps_sq": es2.enter_context(nc.psum_tensor("ps_sq", [128, BLK], F32)),
                "xsq": es2.enter_context(nc.sbuf_tensor("xsq", [128, NC_D, BLK], F32)),
                "a_sb": es2.enter_context(nc.sbuf_tensor("a_sb", [128, BLK], F32)),
                "b_sb": es2.enter_context(nc.sbuf_tensor("b_sb", [128, BLK], F32)),
                "eps": es2.enter_context(nc.sbuf_tensor("eps_col", [128, 1], F32)),
                "prev": {},
            }
            @block.vector
            def _(v):
                v.memset(st["eps"][:, :], EPS).then_inc(v_c.sem, 1)
            v_c.inc()

            xblk = es2.enter_context(nc.sbuf_tensor("xblk", [128, 2, NC_D, BLK], F32))
            lnt = es2.enter_context(nc.sbuf_tensor("lnt", [128, NC_D, BLK], F32))
            ktmp = es2.enter_context(nc.sbuf_tensor("ktmp", [128, 2, BLK], F32))
            khit = es2.enter_context(nc.sbuf_tensor("khit", [128, 2, BLK], BF16))
            klot = es2.enter_context(nc.sbuf_tensor("klot", [128, 2, BLK], BF16))
            ksq = es2.enter_context(nc.sbuf_tensor("ksq", [128, 2, BLK], F32))
            vtmp = es2.enter_context(nc.sbuf_tensor("vtmp", [128, 2, BLK], F32))
            km_t = es2.enter_context(nc.sbuf_tensor("km_t", [2, 1], F32))
            ps_kv = es2.enter_context(nc.psum_tensor("ps_kv", [128, 2, BLK], F32))
            ps_n2 = es2.enter_context(nc.psum_tensor("ps_n2", [2, BLK], F32))

            def proj_sweep(src_ap, nblk, do_q, tagp):
                lnfree = {}
                sweep_fence = v_c.v
                for tb in range(nblk):
                    buf = tb % 2

                    x_c = sems[f"x{buf}"]

                    @block.sync
                    def _(s, tb=tb, buf=buf):
                        if tb >= 2:
                            s.wait_ge(v_c.sem, lnfree[tb - 2])
                        else:
                            s.wait_ge(v_c.sem, sweep_fence)
                        for c in range(NC_D):
                            s.dma_start(out=xblk[:, buf, c, :],
                                        in_=src_ap[c * 128:(c + 1) * 128,
                                                   tb * BLK:(tb + 1) * BLK]
                                        ).then_inc(x_c.sem, 16)
                    x_c.inc(16 * NC_D)

                    ab_ready = _emit_stats(nc, block, es, names, xblk[:, buf], st,
                                           sems, x_c.sem, x_c.v)
                    t_prev = t_c.v

                    @block.vector
                    def _(v, buf=buf, ab_ready=ab_ready, t_prev=t_prev):
                        v.wait_ge(v_c.sem, ab_ready)
                        v.wait_ge(t_c.sem, t_prev)
                        for c in range(NC_D):
                            v.tensor_mul(out=lnt[:, c, :], in0=xblk[:, buf, c, :],
                                         in1=st["a_sb"][:, :])
                        v.drain()
                        for c in range(NC_D):
                            _last = v.tensor_add(out=lnt[:, c, :], in0=lnt[:, c, :],
                                         in1=st["b_sb"][:, :])
                        _last.then_inc(v_c.sem, 1)
                    ln_done = v_c.inc()
                    lnfree[tb] = ln_done

                    kst = sems.setdefault("_kst", {})
                    vst = sems.setdefault("_vst", {})
                    wmat = wq_sb if do_q else wk_sb
                    bcol = bq_sb if do_q else bk_sb
                    mx = qmax2 if do_q else kmax2
                    hist = sems.setdefault("_pskv_hist", {})

                    def emit_n2(pend, mx=mx):
                        pp, pev, pn2read = pend

                        @block.tensor
                        def _(t, pp=pp, pev=pev, pn2read=pn2read):
                            t.wait_ge(s_c.sem, pev)
                            t.wait_ge(v_c.sem, pn2read)
                            _last = t.matmul(ps_n2[:, :], names["onesab_sb"][:, :],
                                     ksq[:, pp % 2, :], start=True, stop=True)
                            _last.then_inc(t_c.sem, 1)
                        n2_mm = t_c.inc()

                        @block.vector
                        def _(v, pp=pp, n2_mm=n2_mm, mx=mx):
                            v.wait_ge(t_c.sem, n2_mm)
                            v.reduce_max(out=km_t[:, 0:1], in_=ps_n2[:, :], axis=AX.X)
                            v.drain()
                            _last = v.tensor_max(out=mx[:, pp:pp + 1],
                                         in0=mx[:, pp:pp + 1], in1=km_t[:, 0:1])
                            _last.then_inc(v_c.sem, 1)
                        sems["_n2read"] = v_c.inc()

                    pend = None
                    for p in range(NP):
                        bank_free = hist.get(p % 2)

                        @block.tensor
                        def _(t, p=p, ln_done=ln_done, bank_free=bank_free):
                            t.wait_ge(v_c.sem, ln_done)
                            t.wait_ge(d_c.sem, wload)
                            if bank_free is not None:
                                t.wait_ge(bank_free[0].sem, bank_free[1])
                            for kc in range(NC_D):
                                _last = t.matmul(ps_kv[:, p % 2, :],
                                         wmat[:, kc, p * 128:(p + 1) * 128],
                                         lnt[:, kc, :], start=(kc == 0),
                                         stop=(kc == NC_D - 1))
                            _last.then_inc(t_c.sem, 1)
                        k_mm = t_c.inc()

                        if pend is not None:
                            emit_n2(pend)

                        @block.scalar
                        def _(s, p=p, k_mm=k_mm, kprev=g_c.v):
                            s.wait_ge(t_c.sem, k_mm)
                            s.wait_ge(g_c.sem, kprev)
                            s.activation(out=ktmp[:, p % 2, :], in_=ps_kv[:, p % 2, :],
                                         func=AF.Identity, bias=bcol[:, p:p + 1]
                                         ).then_inc(s_c.sem, 1)
                            s.drain()
                            _last = s.activation(out=ksq[:, p % 2, :],
                                         in_=ktmp[:, p % 2, :], func=AF.Square)
                            _last.then_inc(s_c.sem, 1)
                        s_c.inc(2)
                        k_evac = s_c.v
                        hist[p % 2] = (s_c, k_evac - 1)
                        pend = (p, k_evac, sems.get("_n2read", 0))

                        prev_st = g_c.v

                        @block.vector
                        def _(v, p=p, k_evac=k_evac, prev_st=prev_st):
                            v.wait_ge(s_c.sem, k_evac - 1)
                            v.wait_ge(g_c.sem, prev_st)
                            v.tensor_copy(out=khit[:, p % 2, :], in_=ktmp[:, p % 2, :])
                            v.drain()
                            v.tensor_sub(out=klot[:, p % 2, :], in0=ktmp[:, p % 2, :],
                                         in1=khit[:, p % 2, :]).then_inc(v_c.sem, 1)
                        hl_done = v_c.inc()

                        @block.gpsimd
                        def _(g, p=p, tb=tb, hl_done=hl_done):
                            g.wait_ge(v_c.sem, hl_done)
                            dh, dl = (qfmh, qfml) if do_q else (kfmh, kfml)
                            g.dma_start(out=dh[p, :, tb * BLK:(tb + 1) * BLK],
                                        in_=khit[:, p % 2, :]).then_inc(g_c.sem, 16)
                            g.dma_start(out=dl[p, :, tb * BLK:(tb + 1) * BLK],
                                        in_=klot[:, p % 2, :]).then_inc(g_c.sem, 16)
                        g_c.inc(32)
                        kst[(tb, p)] = g_c.v
                        kst["last"] = g_c.v if p == NP - 1 else kst.get("last", 0)
                    emit_n2(pend)

                    if not do_q:
                        k_all_evac = s_c.v
                        for stile in range(BLK // 128):
                            for hf in range(2):
                                prev_vevac = v_c.v

                                @block.tensor
                                def _(t, stile=stile, hf=hf, ln_done=ln_done,
                                      k_all_evac=k_all_evac, prev_vevac=prev_vevac):
                                    t.wait_ge(v_c.sem, ln_done)
                                    t.wait_ge(s_c.sem, k_all_evac)
                                    t.wait_ge(v_c.sem, prev_vevac)
                                    for kc in range(NC_D):
                                        t.matmul(ps_kv[:, hf, :],
                                                 lnt[:, kc, stile * 128:(stile + 1) * 128],
                                                 wv_sb[:, kc, hf * 512:(hf + 1) * 512],
                                                 start=(kc == 0), stop=False,
                                                 skip_group_check=True)
                                    _last = t.matmul(ps_kv[:, hf, :], names["onesrow_sb"][0:1, :],
                                             bv_sb[0:1, hf * 512:(hf + 1) * 512],
                                             start=False, stop=True,
                                             skip_group_check=True)
                                    _last.then_inc(t_c.sem, 1)
                                v_mm = t_c.inc()

                                @block.vector
                                def _(v, hf=hf, v_mm=v_mm, vprev=g_c.v):
                                    v.wait_ge(t_c.sem, v_mm)
                                    v.wait_ge(g_c.sem, vprev)
                                    v.tensor_copy(out=vtmp[:, hf, :], in_=ps_kv[:, hf, :]).then_inc(v_c.sem, 1)
                                v_evac = v_c.inc()
                                sems.setdefault("_pskv_hist", {})[hf] = (v_c, v_evac)

                                @block.gpsimd
                                def _(g, tb=tb, stile=stile, hf=hf, v_evac=v_evac):
                                    g.wait_ge(v_c.sem, v_evac)
                                    g.dma_start(
                                        out=vtm[tb * BLK + stile * 128:
                                                tb * BLK + (stile + 1) * 128,
                                                hf * 512:(hf + 1) * 512],
                                        in_=vtmp[:, hf, :]).then_inc(g_c.sem, 16)
                                g_c.inc(16)
                                vst[(tb, stile, hf)] = g_c.v
                                if stile == BLK // 128 - 1 and hf == 1:
                                    vst["last"] = g_c.v

            proj_sweep(x_all, NBLK_B, False, "kv")
            proj_sweep(x_own, NBLK_O, True, "q")
            prod_done_g = g_c.v
            prod_done_v = v_c.v
            prod_done_t = t_c.v
            prod_done_s = s_c.v

        es_mc = contextlib.ExitStack()
        ps_misc = es_mc.enter_context(nc.psum_tensor("ps_misc", [128, 16], F32))

        @block.vector
        def _(v):
            v.wait_ge(v_c.sem, prod_done_v)
            v.tensor_mul(out=mval[:, :], in0=qmax2[:, :], in1=kmax2[:, :]).then_inc(v_c.sem, 1)
        mv1 = v_c.inc()

        @block.scalar
        def _(s):
            s.wait_ge(v_c.sem, mv1)
            s.activation(out=mval[:, :], in_=mval[:, :], func=AF.Sqrt).then_inc(s_c.sem, 1)
        mv2 = s_c.inc()

        @block.vector
        def _(v):
            v.wait_ge(s_c.sem, mv2)
            _last = v.tensor_scalar(out=mval[:, :], in0=mval[:, :], scalar1=-SCALE,
                            scalar2=None, op0=ALU.mult)
            _last.then_inc(v_c.sem, 1)
        mv3 = v_c.inc()

        @block.tensor
        def _(t):
            t.wait_ge(v_c.sem, mv3)
            for h2 in range(2):
                _last = t.matmul(ps_misc[:, h2 * NP:(h2 + 1) * NP],
                         names["h2sel_sb"][:, h2, :], mval[:, :],
                         start=True, stop=True)
            _last.then_inc(t_c.sem, 1)
        mc_mm = t_c.inc()

        @block.vector
        def _(v):
            v.wait_ge(t_c.sem, mc_mm)
            _last = v.tensor_copy(
                out=mcol[:, :].rearrange("p (pp hh) -> p hh pp", hh=2),
                in_=ps_misc[:, :].rearrange("p (hh pp) -> p hh pp", hh=2))
            _last.then_inc(v_c.sem, 1)
        mcol_done = v_c.inc()

        es_mc.close()
        ktile = es.enter_context(nc.sbuf_tensor("ktile", [128, 2, 2, 128], BF16))
        vt = es.enter_context(nc.sbuf_tensor("vt", [128, 2, NKC, 132], F32))
        qth = es.enter_context(nc.sbuf_tensor("qth", [128, 2, 2, OWN], BF16))
        qtl = es.enter_context(nc.sbuf_tensor("qtl", [64, 2, 2, OWN], BF16))
        pt = es.enter_context(nc.sbuf_tensor("pt", [128, 2, 2, QGRP], F32))
        otA = es.enter_context(nc.sbuf_tensor("otA", [64, OWN], F32))
        otB = es.enter_context(nc.sbuf_tensor("otB", [64, OWN], F32))
        ot_h = [otA, otB]
        lsb = es.enter_context(nc.sbuf_tensor("lsb", [65, 2, OWN], F32))
        sqsb = es.enter_context(nc.sbuf_tensor("sqsb", [64, QGRP], F32))
        nsqsb = es.enter_context(nc.sbuf_tensor("nsqsb", [1, 2, OWN], F32))
        ps_sc = es.enter_context(nc.psum_tensor("ps_sc", [128, 2, 2, QGRP], F32))
        ps_po = es.enter_context(nc.psum_tensor("ps_po", [65, 2, QGRP], F32))
        ps_nsq = es.enter_context(nc.psum_tensor("ps_nsq", [1, QGRP], F32))

        @block.vector
        def _(v):
            v.memset(vt[:, :, :, 64:65].rearrange("p b k o -> p (b k o)"), 1.0)
            v.memset(vt[:, :, :, 129:130].rearrange("p b k o -> p (b k o)"), 1.0).then_inc(v_c.sem, 1)
        ones_done = v_c.inc()

        pair_free = {}
        ot_store = {}
        pair_loaded = {}

        def emit_pair_loads(p):
            pb = p % 2
            p_c = sems[f"p{pb}"]

            @block.sync
            def _(s, p=p, pb=pb):
                s.wait_ge(g_c.sem, prod_done_g)
                if p == 0:
                    s.wait_ge(t_c.sem, prod_done_t)
                    s.wait_ge(s_c.sem, prod_done_s)
                    s.wait_ge(v_c.sem, prod_done_v)
                if p >= 2:
                    s.wait_ge(t_c.sem, pair_free[p - 2])
                for h2 in range(2):
                    s.dma_start(out=qth[0:64, pb, h2, :],
                                in_=qfmh[p, h2 * 64:(h2 + 1) * 64, :]).then_inc(p_c.sem, 16)
                    s.dma_start(out=qth[64:128, pb, h2, :],
                                in_=qfmh[p, h2 * 64:(h2 + 1) * 64, :]).then_inc(p_c.sem, 16)
                    s.dma_start(out=qtl[:, pb, h2, :],
                                in_=qfml[p, h2 * 64:(h2 + 1) * 64, :]).then_inc(p_c.sem, 16)
                for kc in range(NKC):
                    for h2 in range(2):
                        s.dma_start(out=vt[:, pb, kc, h2 * 65:h2 * 65 + 64],
                                    in_=vtm[kc * 128:(kc + 1) * 128,
                                            (2 * p + h2) * HD:(2 * p + h2 + 1) * HD]
                                    ).then_inc(p_c.sem, 16)
            p_c.inc(16 * (6 + 2 * NKC))
            pair_loaded[p] = p_c.v

        emit_pair_loads(0)
        for p in range(NP):
            pb = p % 2
            kvq_loaded = pair_loaded[p]

            for qg in range(NQG):
                evac_prev = v_c.v
                sq_prev = s_c.v

                @block.tensor
                def _(t, p=p, pb=pb, kvq_loaded=kvq_loaded, evac_prev=evac_prev,
                      sq_prev=sq_prev):
                    t.wait_ge(sems[f"p{pb}"].sem, kvq_loaded)
                    t.wait_ge(v_c.sem, max(mcol_done, ones_done, evac_prev))
                    t.wait_ge(s_c.sem, sq_prev)
                    t.nop().then_inc(t_c.sem, 1)
                t_c.inc()

                exp_base = s_c.v
                zb = t_c.v
                def scores_done_val(kc, zb=zb):
                    return zb + 1 if kc == 0 else zb + 2 * kc

                kc_loaded = {}
                prev_sd = sems.get("_prev_scores_done")
                k_base = {0: sems["k0"].v, 1: sems["k1"].v}

                @block.sync
                def _(s, p=p, prev_sd=prev_sd):
                    for kc in range(NKC):
                        k_c = sems[f"k{kc % 2}"]
                        if kc >= 2:
                            s.wait_ge(t_c.sem, scores_done_val(kc - 2))
                        elif prev_sd is not None:
                            s.wait_ge(t_c.sem, prev_sd(NKC - 2 + kc))
                        for h2 in range(2):
                            s.dma_start(out=ktile[0:64, kc % 2, h2, :],
                                        in_=kfmh[p, h2 * 64:(h2 + 1) * 64,
                                                 kc * 128:(kc + 1) * 128]
                                        ).then_inc(k_c.sem, 16)
                            s.dma_start(out=ktile[64:128, kc % 2, h2, :],
                                        in_=kfml[p, h2 * 64:(h2 + 1) * 64,
                                                 kc * 128:(kc + 1) * 128]
                                        ).then_inc(k_c.sem, 16)
                for kc in range(NKC):
                    kc_loaded[kc] = sems[f"k{kc % 2}"].inc(64)
                if qg == 0 and p + 1 < NP:
                    emit_pair_loads(p + 1)

                @block.tensor
                def _(t, p=p, pb=pb, qg=qg, exp_base=exp_base,
                      kc_loaded=dict(kc_loaded)):
                    for kc in range(NKC):
                        kb = kc % 2
                        t.wait_ge(sems[f"k{kb}"].sem, kc_loaded[kc])
                        if kc >= 2:
                            t.wait_ge(s_c.sem, exp_base + 2 * (kc - 1))
                        for h2 in range(2):
                            t.matmul(ps_sc[:, kb, h2, :], ktile[:, kb, h2, :],
                                     qth[:, pb, h2, qg * QGRP:(qg + 1) * QGRP],
                                     start=True, stop=False, skip_group_check=True)
                        t.matmul(ps_sc[:, kb, 0, :], ktile[0:64, kb, 0, :],
                                 qtl[:, pb, 0, qg * QGRP:(qg + 1) * QGRP],
                                 start=False, stop=True, skip_group_check=True)
                        t.matmul(ps_sc[:, kb, 1, :], ktile[0:64, kb, 1, :],
                                 qtl[:, pb, 1, qg * QGRP:(qg + 1) * QGRP],
                                 start=False, stop=True, skip_group_check=True
                                 ).then_inc(t_c.sem, 1)
                        if kc >= 1:
                            t.wait_ge(s_c.sem, exp_base + 2 * kc)
                            for h2 in range(2):
                                last = t.matmul(
                                    ps_po[:, h2, :],
                                    vt[:, pb, kc - 1, h2 * 65:(h2 + 1) * 65],
                                    pt[:, (kc - 1) % 2, h2, :],
                                    start=(kc - 1 == 0), stop=(kc - 1 == NKC - 1))
                            last.then_inc(t_c.sem, 1)
                scores_done = {}
                for kc in range(NKC):
                    scores_done[kc] = t_c.inc()
                    assert scores_done[kc] == scores_done_val(kc)
                    if kc >= 1:
                        t_c.inc()

                @block.scalar
                def _(s, p=p, exp_base=exp_base, zb=zb):
                    for kc in range(NKC):
                        kb = kc % 2
                        s.wait_ge(t_c.sem, scores_done_val(kc))
                        for h2 in range(2):
                            s.activation(out=pt[:, kb, h2, :], in_=ps_sc[:, kb, h2, :],
                                         func=AF.Exp,
                                         bias=mcol[:, 2 * p + h2:2 * p + h2 + 1],
                                         scale=SCALE).then_inc(s_c.sem, 1)
                s_c.inc(2 * NKC)
                sems["_prev_scores_done"] = scores_done_val

                @block.tensor
                def _(t, p=p, pb=pb, exp_base=exp_base):
                    t.wait_ge(s_c.sem, exp_base + 2 * NKC)
                    for h2 in range(2):
                        _last = t.matmul(ps_po[:, h2, :],
                                 vt[:, pb, NKC - 1, h2 * 65:(h2 + 1) * 65],
                                 pt[:, (NKC - 1) % 2, h2, :],
                                 start=False, stop=True)
                    _last.then_inc(t_c.sem, 1)
                av_done = t_c.inc()

                @block.vector
                def _(v, p=p, qg=qg, av_done=av_done, ot_prev=ot_store.get(p - 1)):
                    v.wait_ge(t_c.sem, av_done)
                    if ot_prev is not None:
                        v.wait_ge(g_c.sem, ot_prev)
                    for h2 in range(2):
                        v.tensor_copy(out=ot_h[h2][:, qg * QGRP:(qg + 1) * QGRP],
                                      in_=ps_po[0:64, h2, :])
                    for h2 in range(2):
                        _last = v.tensor_copy(out=lsb[64:65, h2, qg * QGRP:(qg + 1) * QGRP],
                                              in_=ps_po[64:65, h2, :])
                    _last.then_inc(v_c.sem, 1)
                oc_done = v_c.inc()

                for h2 in range(2):
                    prev_mm = t_c.v
                    prev_cp = v_c.v

                    @block.scalar
                    def _(s, h2=h2, oc_done=oc_done, prev_mm=prev_mm):
                        s.wait_ge(v_c.sem, oc_done)
                        s.wait_ge(t_c.sem, prev_mm)
                        s.activation(out=sqsb[:, :],
                                     in_=ps_po[0:64, h2, :],
                                     func=AF.Square).then_inc(s_c.sem, 1)
                    sq_done = s_c.inc()

                    @block.tensor
                    def _(t, sq_done=sq_done, prev_cp=prev_cp):
                        t.wait_ge(s_c.sem, sq_done)
                        t.wait_ge(v_c.sem, prev_cp)
                        t.matmul(ps_nsq[0:1, :], names["ones128_sb"][0:64, 0:1],
                                 sqsb[:, :], start=True, stop=True).then_inc(t_c.sem, 1)
                    nsq_mm = t_c.inc()

                    @block.vector
                    def _(v, h2=h2, qg=qg, nsq_mm=nsq_mm):
                        v.wait_ge(t_c.sem, nsq_mm)
                        v.tensor_copy(out=nsqsb[0:1, h2, qg * QGRP:(qg + 1) * QGRP],
                                      in_=ps_nsq[0:1, :]).then_inc(v_c.sem, 1)
                    v_c.inc()
                evac_done_here = s_c.v

            pair_free[p] = t_c.v
            last_ev = v_c.v

            @block.gpsimd
            def _(g, p=p, last_ev=last_ev):
                g.wait_ge(v_c.sem, last_ev)
                g.dma_start(out=otp_out[p, 0:64, :], in_=otA[:, :]).then_inc(g_c.sem, 16)
                g.dma_start(out=otp_out[p, 64:128, :], in_=otB[:, :]).then_inc(g_c.sem, 16)
                for h2 in range(2):
                    g.dma_start(out=l_out[2 * p + h2:2 * p + h2 + 1, :],
                                in_=lsb[64:65, h2, :]).then_inc(g_c.sem, 16)
                    g.dma_start(out=nsq_out[2 * p + h2:2 * p + h2 + 1, :],
                                in_=nsqsb[0:1, h2, :]).then_inc(g_c.sem, 16)
            ot_store[p] = g_c.inc(16 * 6)

        @block.gpsimd
        def _(g):
            g.wait_ge(g_c.sem, g_c.v)
    return nc



def build_l3():
    nc = bass.Bass("TRN2", target_bir_lowering=False, debug=False)
    x1r = nc.dram_tensor("x1r", [D, T], F32, kind="ExternalInput").ap()
    otr = nc.dram_tensor("otr", [NP, 128, T], F32, kind="ExternalInput").ap()
    lg = nc.dram_tensor("lg", [H, TOK], F32, kind="ExternalInput").ap()
    nsqg = nc.dram_tensor("nsqg", [H, TOK], F32, kind="ExternalInput").ap()
    lrot = nc.dram_tensor("lrot", [H, T], F32, kind="ExternalInput").ap()
    wo = nc.dram_tensor("wo", [NP, 128, D], F32, kind="ExternalInput").ap()
    bo = nc.dram_tensor("bo", [H, D], F32, kind="ExternalInput").ap()
    gate = nc.dram_tensor("gate", [1, H], F32, kind="ExternalInput").ap()
    hbm = _declare_ffn_hbm(nc)
    y_out = nc.dram_tensor("y_own", [D, OWN], F32, kind="ExternalOutput").ap()
    x2scr = nc.dram_tensor("x2_scr", [D, OWN], F32, kind="Internal").ap()
    linvscr = nc.dram_tensor("linv_scr", [H, T], F32, kind="Internal").ap()

    es = contextlib.ExitStack()
    with es:
        sems = _mk_sems(nc, es)
        t_c, s_c, v_c, d_c, g_c = (sems[k] for k in ("t", "s", "v", "d", "g"))
        block = es.enter_context(nc.Block())
        names = _load_consts(nc, block, es, sems)

        st = {
            "ps_sum": es.enter_context(nc.psum_tensor("ps_sum", [128, BLK], F32)),
            "ps_sq": es.enter_context(nc.psum_tensor("ps_sq", [128, BLK], F32)),
            "xsq": es.enter_context(nc.sbuf_tensor("xsq", [128, NC_D, BLK], F32)),
            "a_sb": es.enter_context(nc.sbuf_tensor("a_sb", [128, BLK], F32)),
            "b_sb": es.enter_context(nc.sbuf_tensor("b_sb", [128, BLK], F32)),
            "eps": es.enter_context(nc.sbuf_tensor("eps_col", [128, 1], F32)),
            "prev": {},
        }
        gacc = es.enter_context(nc.sbuf_tensor("gacc", [128, NC_D], F32))
        rtmp = es.enter_context(nc.sbuf_tensor("rtmp", [128, NC_D], F32))
        gf = es.enter_context(nc.sbuf_tensor("gf", [1, 1], F32))
        AB_sb = es.enter_context(nc.sbuf_tensor("AB", [1, 2], F32))
        bcol = es.enter_context(nc.sbuf_tensor("bcol", [128, NC_D], F32))

        @block.vector
        def _(v):
            v.memset(gacc[:, :], 0.0)
            v.memset(st["eps"][:, :], EPS).then_inc(v_c.sem, 1)
        v_c.inc()

        with contextlib.ExitStack() as es2:
            gate_sb = es2.enter_context(nc.sbuf_tensor("gate_sb", [1, H], F32))
            den_sb = es2.enter_context(nc.sbuf_tensor("den_sb", [1, H], F32))
            wfac = es2.enter_context(nc.sbuf_tensor("wfac", [1, H], F32))
            wcol = es2.enter_context(nc.sbuf_tensor("wcol", [128, NP], F32))
            gcol16 = es2.enter_context(nc.sbuf_tensor("gcol16", [16, 1], F32))
            bo_sb = es2.enter_context(nc.sbuf_tensor("bo_sb", [16, D], F32))
            wo_sb = es2.enter_context(nc.sbuf_tensor("wo_sb", [128, NP, D], F32))
            ps_a = es2.enter_context(nc.psum_tensor("ps_a", [128, 2, BLK], F32))
            ps_w = es2.enter_context(nc.psum_tensor("ps_w", [128, 16], F32))

            @block.sync
            def _(s):
                s.dma_start(out=gate_sb[0:1, :], in_=gate[0:1, :]).then_inc(d_c.sem, 16)
                s.dma_start(out=bo_sb[:, :], in_=bo[:, :]).then_inc(d_c.sem, 16)
                s.dma_start(out=gcol16[:, 0:1],
                            in_=gate.rearrange("o h -> h o"), allow_slow_non_contiguous=True).then_inc(d_c.sem, 16)
                for p in range(NP):
                    s.dma_start(out=wo_sb[:, p, :], in_=wo[p, :, :]).then_inc(d_c.sem, 16)
            d_c.inc(16 * (3 + NP))
            wo_load = d_c.v

            es3 = contextlib.ExitStack()
            lgt = es3.enter_context(nc.sbuf_tensor("lgt", [128, TOK * H // 128], F32))
            nst = es3.enter_context(nc.sbuf_tensor("nst", [128, TOK * H // 128], F32))
            lrt = es3.enter_context(nc.sbuf_tensor("lrt", [128, T * H // 128], F32))
            nred = es3.enter_context(nc.sbuf_tensor("nred", [128, 1], F32))
            den16 = es3.enter_context(nc.sbuf_tensor("den16", [16, 1], F32))

            k1_c = sems["k1"]

            @block.sync
            def _(s):
                s.dma_start(out=lgt[:, :],
                            in_=lg.rearrange("h (c t) -> (h c) t", c=8)).then_inc(k1_c.sem, 16)
                s.dma_start(out=nst[:, :],
                            in_=nsqg.rearrange("h (c t) -> (h c) t", c=8)).then_inc(k1_c.sem, 16)
                s.dma_start(out=lrt[:, :],
                            in_=lrot.rearrange("h (c t) -> (h c) t", c=8)).then_inc(k1_c.sem, 16)
            k1_c.inc(16 * 3)
            lg_load = k1_c.v

            @block.vector
            def _(v):
                v.wait_ge(sems["k1"].sem, lg_load)
                v.reciprocal(out=lgt[:, :], in_=lgt[:, :])
                v.drain()
                v.reciprocal(out=lrt[:, :], in_=lrt[:, :])
                v.drain().then_inc(v_c.sem, 1)
            linv_done = v_c.inc()

            @block.gpsimd
            def _(g):
                g.wait_ge(v_c.sem, linv_done)
                g.dma_start(out=linvscr.rearrange("h (c t) -> (h c) t", c=8),
                            in_=lrt[:, :]).then_inc(g_c.sem, 16)
            g_c.inc(16)
            linv_stored = g_c.v

            @block.scalar
            def _(s):
                s.wait_ge(sems["k1"].sem, lg_load)
                s.activation(out=nst[:, :], in_=nst[:, :], func=AF.Sqrt
                             ).then_inc(s_c.sem, 1)
            nsq_sqrt = s_c.inc()

            @block.vector
            def _(v):
                v.wait_ge(s_c.sem, nsq_sqrt)
                v.wait_ge(v_c.sem, linv_done)
                v.tensor_mul(out=nst[:, :], in0=nst[:, :], in1=lgt[:, :])
                v.drain()
                v.reduce_sum(out=nred[:, 0:1], in_=nst[:, :], axis=AX.X
                             ).then_inc(v_c.sem, 1)
            nred_done = v_c.inc()

            @block.tensor
            def _(t):
                t.wait_ge(v_c.sem, nred_done)
                t.matmul(ps_w[0:16, 0:1], names["hsel_sb"][:, :], nred[:, 0:1],
                         start=True, stop=True).then_inc(t_c.sem, 1)
            den_mm0 = t_c.inc()

            @block.vector
            def _(v):
                v.wait_ge(t_c.sem, den_mm0)
                v.tensor_copy(out=den16[:, 0:1], in_=ps_w[0:16, 0:1]).then_inc(v_c.sem, 1)
            den16_done = v_c.inc()

            @block.gpsimd
            def _(g):
                g.wait_ge(v_c.sem, den16_done)
                g.dma_start(out=den_sb[0:1, :], in_=den16[:, 0:1]
                            ).then_inc(sems["k0"].sem, 16)
            den_row_done = sems["k0"].inc(16)

            @block.vector
            def _(v):
                v.wait_ge(sems["k0"].sem, den_row_done)
                v.wait_ge(d_c.sem, wo_load)
                v.tensor_scalar(out=den_sb[0:1, :], in0=den_sb[0:1, :],
                                scalar1=1.0 / TOK, scalar2=1e-5, op0=ALU.mult,
                                op1=ALU.max)
                v.drain()
                v.reciprocal(out=den_sb[0:1, :], in_=den_sb[0:1, :])
                v.drain()
                v.tensor_scalar(out=gate_sb[0:1, :], in0=gate_sb[0:1, :], scalar1=0.0,
                                scalar2=1.0, op0=ALU.max, op1=ALU.min)
                v.tensor_scalar(out=gcol16[:, 0:1], in0=gcol16[:, 0:1], scalar1=0.0,
                                scalar2=1.0, op0=ALU.max, op1=ALU.min)
                v.drain()
                v.tensor_mul(out=wfac[0:1, :], in0=gate_sb[0:1, :], in1=den_sb[0:1, :])
                v.drain()
                v.tensor_scalar(out=wfac[0:1, :], in0=wfac[0:1, :], scalar1=1.0 / H,
                                scalar2=None, op0=ALU.mult)
                v.drain()
                v.drain().then_inc(v_c.sem, 1)
            wfac_done = v_c.inc()

            @block.tensor
            def _(t):
                t.wait_ge(v_c.sem, wfac_done)
                t.matmul(ps_w[:, 0:NP], names["rsa_sb"][0:1, :],
                         wfac[0:1, :].rearrange("o (pp hh) -> o hh pp", hh=2)[:, 0, :],
                         start=True, stop=False, skip_group_check=True)
                _last = t.matmul(ps_w[:, 0:NP], names["rsb_sb"][0:1, :],
                         wfac[0:1, :].rearrange("o (pp hh) -> o hh pp", hh=2)[:, 1, :],
                         start=False, stop=True, skip_group_check=True)
                _last.then_inc(t_c.sem, 1)
            wc_mm = t_c.inc()

            @block.vector
            def _(v):
                v.wait_ge(t_c.sem, wc_mm)
                v.tensor_copy(out=wcol[:, :], in_=ps_w[:, 0:NP])
                v.drain()
                for p in range(NP):
                    _last = v.tensor_scalar(out=wo_sb[:, p, :], in0=wo_sb[:, p, :],
                                    scalar1=wcol[:, p:p + 1], scalar2=None,
                                    op0=ALU.mult)
                _last.then_inc(v_c.sem, 1)
            wo_scaled = v_c.inc()

            @block.vector
            def _(v):
                v.wait_ge(v_c.sem, wfac_done)
                _last = v.tensor_scalar(out=bo_sb[:, :], in0=bo_sb[:, :],
                                scalar1=gcol16[:, 0:1], scalar2=None, op0=ALU.mult)
                _last.then_inc(v_c.sem, 1)
            bo_scaled = v_c.inc()

            @block.tensor
            def _(t):
                t.wait_ge(v_c.sem, bo_scaled)
                for c in range(NC_D):
                    _last = t.matmul(ps_w[0:128, 8 + c:8 + c + 1] if False else ps_a[:, 1, c:c + 1],
                             bo_sb[:, c * 128:(c + 1) * 128],
                             names["ones128_sb"][0:16, 0:1],
                             start=(c == 0), stop=(c == NC_D - 1))
                _last.then_inc(t_c.sem, 1)
            bc_mm = t_c.inc()

            @block.vector
            def _(v):
                v.wait_ge(t_c.sem, bc_mm)
                _last = v.tensor_scalar(out=bcol[:, :], in0=ps_a[:, 1, 0:NC_D], scalar1=1.0 / H,
                                scalar2=None, op0=ALU.mult)
                _last.then_inc(v_c.sem, 1)
            bcol_done = v_c.inc()

            phA_fence = (t_c.v, s_c.v, v_c.v)
            es3.close()
            xblk = es2.enter_context(nc.sbuf_tensor("xblk", [128, 2, NC_D, BLK], F32))
            oblk = es2.enter_context(nc.sbuf_tensor("oblk", [128, 2, NP, BLK], F32))
            x2blk = es2.enter_context(nc.sbuf_tensor("x2blk", [128, NC_D, BLK], F32))
            lnt = es2.enter_context(nc.sbuf_tensor("lnt", [128, NC_D, BLK], F32))
            l2sb = es2.enter_context(nc.sbuf_tensor("l2sb", [2, 2, NP, BLK], F32))
            ps_lb = es2.enter_context(nc.psum_tensor("ps_lb", [128, 2, BLK], F32))
            b_free = {}
            for tb in range(NBLK_B):
                buf = tb % 2

                x_c = sems[f"x{buf}"]

                @block.sync
                def _(s, tb=tb, buf=buf):
                    s.wait_ge(g_c.sem, linv_stored)
                    if tb >= 2:
                        s.wait_ge(v_c.sem, b_free[tb - 2])
                    else:
                        s.wait_ge(t_c.sem, phA_fence[0])
                        s.wait_ge(s_c.sem, phA_fence[1])
                        s.wait_ge(v_c.sem, phA_fence[2])
                    for c in range(NC_D):
                        s.dma_start(out=xblk[:, buf, c, :],
                                    in_=x1r[c * 128:(c + 1) * 128, tb * BLK:(tb + 1) * BLK]
                                    ).then_inc(x_c.sem, 16)
                    for p in range(NP):
                        s.dma_start(out=oblk[:, buf, p, :],
                                    in_=otr[p, :, tb * BLK:(tb + 1) * BLK]
                                    ).then_inc(x_c.sem, 16)
                        s.dma_start(out=l2sb[:, buf, p, :],
                                    in_=linvscr[2 * p:2 * p + 2, tb * BLK:(tb + 1) * BLK]
                                    ).then_inc(x_c.sem, 16)
                x_c.inc(16 * (NC_D + 2 * NP))
                blk_loaded = x_c.v

                div_done = {}
                for p in range(NP):
                    prev_div = div_done.get(p - 2, 0)

                    @block.tensor
                    def _(t, p=p, buf=buf, blk_loaded=blk_loaded, prev_div=prev_div):
                        t.wait_ge(sems[f"x{buf}"].sem, blk_loaded)
                        if p >= 2:
                            t.wait_ge(v_c.sem, prev_div)
                        t.matmul(ps_lb[:, p % 2, :], names["rsab_sb"][:, :],
                                 l2sb[:, buf, p, :], start=True, stop=True
                                 ).then_inc(t_c.sem, 1)
                    lb_mm = t_c.inc()

                    @block.vector
                    def _(v, p=p, buf=buf, lb_mm=lb_mm):
                        v.wait_ge(t_c.sem, lb_mm)
                        v.tensor_mul(out=oblk[:, buf, p, :], in0=oblk[:, buf, p, :],
                                     in1=ps_lb[:, p % 2, :]).then_inc(v_c.sem, 1)
                    div_done[p] = v_c.inc()
                all_div = div_done[NP - 1]

                pj_evac = {}
                scr_g_prev = g_c.v
                for c in range(NC_D):
                    @block.tensor
                    def _(t, c=c, buf=buf, blk_loaded=blk_loaded, pj_evac=pj_evac,
                          tb=tb, scr_g_prev=scr_g_prev, all_div=all_div):
                        t.wait_ge(sems[f"x{buf}"].sem, blk_loaded)
                        t.wait_ge(v_c.sem, max(wo_scaled, bcol_done, all_div))
                        if c >= 2:
                            t.wait_ge(v_c.sem, pj_evac[c - 2])
                        elif tb > 0:
                            t.wait_ge(v_c.sem, b_free[tb - 1])
                            t.wait_ge(g_c.sem, scr_g_prev)
                        for p in range(NP):
                            _last = t.matmul(ps_a[:, c % 2, :], wo_sb[:, p, c * 128:(c + 1) * 128],
                                     oblk[:, buf, p, :], start=(p == 0),
                                     stop=(p == NP - 1))
                        _last.then_inc(t_c.sem, 1)
                    pj_mm = t_c.inc()

                    @block.scalar
                    def _(s, c=c, pj_mm=pj_mm):
                        s.wait_ge(t_c.sem, pj_mm)
                        s.activation(out=x2blk[:, c, :], in_=ps_a[:, c % 2, :],
                                     func=AF.Identity, bias=bcol[:, c:c + 1]
                                     ).then_inc(s_c.sem, 1)
                    pj_act = s_c.inc()

                    @block.vector
                    def _(v, c=c, buf=buf, pj_act=pj_act):
                        v.wait_ge(s_c.sem, pj_act)
                        _last = v.tensor_add(out=x2blk[:, c, :], in0=x2blk[:, c, :],
                                     in1=xblk[:, buf, c, :])
                        _last.then_inc(v_c.sem, 1)
                    pj_evac[c] = v_c.inc()
                x2_done = v_c.v

                if tb < NBLK_O:
                    @block.gpsimd
                    def _(g, tb=tb, x2_done=x2_done):
                        g.wait_ge(v_c.sem, x2_done)
                        for c in range(NC_D):
                            g.dma_start(out=x2scr[c * 128:(c + 1) * 128,
                                                  tb * BLK:(tb + 1) * BLK],
                                        in_=x2blk[:, c, :]).then_inc(g_c.sem, 16)
                    g_c.inc(16 * NC_D)

                ab_ready = _emit_stats(nc, block, es, names, x2blk, st, sems,
                                       v_c.sem, x2_done)
                ln_done = _emit_ln_apply(block, x2blk, lnt, st["a_sb"], st["b_sb"],
                                         sems, ab_ready)

                @block.vector
                def _(v, ln_done=ln_done):
                    v.wait_ge(v_c.sem, ln_done)
                    for c in range(NC_D):
                        v.reduce_sum(out=rtmp[:, c:c + 1], in_=lnt[:, c, :], axis=AX.X)
                    v.drain()
                    v.tensor_add(out=gacc[:, :], in0=gacc[:, :], in1=rtmp[:, :]).then_inc(v_c.sem, 1)
                b_free[tb] = v_c.inc()
            scr_done_g = g_c.v

        (w_sb, b_sbw, gw_sb, gb_sb, arsb, wload) = _ffn_weight_loads(
            nc, block, es, hbm, sems,
            fence=(t_c.v, s_c.v, v_c.v, g_c.v))
        psg = es.enter_context(nc.psum_tensor("ps_gc", [128, 4], F32))
        gf_done = _emit_gcnet(nc, block, es, names, gacc, gw_sb, gb_sb, gf, psg,
                              sems, b_free[NBLK_B - 1], float(T), "3", wload=wload)
        _emit_ab_scalars(block, gf, arsb, AB_sb, sems, gf_done, wload)
        fence3 = (t_c.v, s_c.v, v_c.v)

        gfin = _emit_ffn_block_sweep(nc, block, es, names, w_sb, b_sbw, AB_sb, st,
                                     sems, x2scr, y_out, 0.5,
                                     (g_c.sem, scr_done_g), "3", fence=fence3,
                                     wload=wload)

        @block.gpsimd
        def _(g):
            g.wait_ge(g_c.sem, gfin)
    return nc



_CACHE = {}


def _get(name, builder):
    if name not in _CACHE:
        _CACHE[name] = builder()
    return _CACHE[name]


def _prep_ffn(inp):
    f = np.float32
    w = {}
    for k in FFN_KEYS:
        src = "L0_ffn_adaptive_range" if k == "ar" else f"L0_ffn_{k}"
        w[k] = np.ascontiguousarray(np.asarray(inp[src], f))
    return w


def run_l1(inp):
    f = np.float32
    x0 = np.asarray(inp["L0_attn_hidden_states"], dtype=f)
    x0fm = [np.ascontiguousarray(x0[b].T) for b in range(B)]
    ffn_w = _prep_ffn(inp)
    nc1 = _get("l1", build_l1)
    in_maps = []
    for c in range(8):
        b, roff = c // 4, (c % 4) * OWN
        m = {"x_all": x0fm[b],
             "x_own": np.ascontiguousarray(x0fm[b][:, roff:roff + OWN])}
        m.update(ffn_w)
        in_maps.append(m)
    r1 = run_bass_kernel_spmd(nc1, in_maps, core_ids=list(range(8)))
    x1_own = [r1.results[c]["x1_own"] for c in range(8)]
    return [np.ascontiguousarray(np.concatenate(x1_own[4 * b:4 * b + 4], axis=1))
            for b in range(B)]


def run_l2(inp, x1fm):
    f = np.float32
    wq_all = np.ascontiguousarray(
        np.asarray(inp["L0_attn_Wq"], f).transpose(1, 0, 2).reshape(D, D))
    wk_all = np.ascontiguousarray(
        np.asarray(inp["L0_attn_Wk"], f).transpose(1, 0, 2).reshape(D, D))
    wv_all = np.ascontiguousarray(
        np.asarray(inp["L0_attn_Wv"], f).transpose(1, 0, 2).reshape(D, D))
    bq_all = np.ascontiguousarray(np.asarray(inp["L0_attn_bq"], f).reshape(D))
    bk_all = np.ascontiguousarray(np.asarray(inp["L0_attn_bk"], f).reshape(D))
    bv_all = np.ascontiguousarray(np.asarray(inp["L0_attn_bv"], f).reshape(D))
    nc2 = _get("l2", build_l2)
    in_maps = []
    for c in range(8):
        b, roff = c // 4, (c % 4) * OWN
        in_maps.append({
            "x_all": x1fm[b],
            "x_own": np.ascontiguousarray(x1fm[b][:, roff:roff + OWN]),
            "wq": wq_all, "wk": wk_all, "wv": wv_all,
            "bq": bq_all, "bk": bk_all, "bv": bv_all,
        })
    r2 = run_bass_kernel_spmd(nc2, in_maps, core_ids=list(range(8)))
    otp = [r2.results[c]["otp"] for c in range(8)]
    ot_full = [np.ascontiguousarray(np.concatenate(otp[4 * b:4 * b + 4], axis=2))
               for b in range(B)]
    l_byb = [np.ascontiguousarray(np.concatenate(
        [r2.results[c]["l_out"] for c in range(4 * b, 4 * b + 4)], axis=1))
        for b in range(B)]
    nsq_byb = [np.ascontiguousarray(np.concatenate(
        [r2.results[c]["nsq_out"] for c in range(4 * b, 4 * b + 4)], axis=1))
        for b in range(B)]
    return ot_full, l_byb, nsq_byb


def run_l3(inp, x1fm, ot_full, l_byb, nsq_byb):
    f = np.float32
    wo_p = np.ascontiguousarray(np.asarray(inp["L0_attn_Wo"], f).reshape(NP, 128, D))
    bo_in = np.ascontiguousarray(np.asarray(inp["L0_attn_bo"], f))
    gate_in = np.ascontiguousarray(np.asarray(inp["L0_attn_gate"], f).reshape(1, H))
    ffn_w = _prep_ffn(inp)
    nc3 = _get("l3", build_l3)
    lg = np.ascontiguousarray(np.concatenate(l_byb, axis=1))
    nsqg = np.ascontiguousarray(np.concatenate(nsq_byb, axis=1))
    in_maps = []
    for c in range(8):
        b, roff = c // 4, (c % 4) * OWN
        m = {"x1r": np.ascontiguousarray(np.roll(x1fm[b], -roff, axis=1)),
             "otr": np.ascontiguousarray(np.roll(ot_full[b], -roff, axis=2)),
             "lg": lg, "nsqg": nsqg,
             "lrot": np.ascontiguousarray(np.roll(l_byb[b], -roff, axis=1)),
             "wo": wo_p, "bo": bo_in, "gate": gate_in}
        m.update(ffn_w)
        in_maps.append(m)
    r3 = run_bass_kernel_spmd(nc3, in_maps, core_ids=list(range(8)))
    return [r3.results[c]["y_own"] for c in range(8)]


def kernel(**inp):
    x1fm = run_l1(inp)
    ot_full, l_byb, nsq_byb = run_l2(inp, x1fm)
    y_own = run_l3(inp, x1fm, ot_full, l_byb, nsq_byb)
    y = np.empty((B, T, D), dtype=np.float32)
    for c in range(8):
        b, roff = c // 4, (c % 4) * OWN
        y[b, roff:roff + OWN, :] = y_own[c].T
    return y


# version 08 (final)
# speedup vs baseline: 1.0742x; 1.0742x over previous
import contextlib
import math

import numpy as np

import concourse.bass as bass
import concourse.mybir as mybir
from concourse.bass_utils import run_bass_kernel_spmd

F32 = mybir.dt.float32
BF16 = mybir.dt.bfloat16
AF = mybir.ActivationFunctionType
ALU = mybir.AluOpType
AX = mybir.AxisListType

B, T, D, H = 2, 8192, 1024, 16
HD = D // H
NP = H // 2
SCALE = 1.0 / math.sqrt(HD)
TOK = B * T
OWN = TOK // 8
NC_D = D // 128
BLK = 512
NBLK_B = T // BLK
NBLK_O = OWN // BLK
EPS = 1e-5
NKC = T // 128
QGRP = 512
NQT = QGRP // 128
NQG = OWN // QGRP

FFN_KEYS = ["cn_w1", "cn_b1", "cn_w2", "cn_b2", "cn_w3", "cn_b3", "cn_w4",
            "cn_b4", "gc_w1", "gc_b1", "gc_w2", "gc_b2", "gc_w3", "gc_b3", "ar"]


class Ctr:
    def __init__(self, sem):
        self.sem = sem
        self.v = 0

    def inc(self, n=1):
        self.v += n
        return self.v


def _mk_sems(nc, es):
    return {k: Ctr(es.enter_context(nc.semaphore(f"sem_{k}")))
            for k in ("t", "s", "v", "d", "g", "c", "x0", "x1", "p0", "p1", "k0", "k1")}


def _declare_ffn_hbm(nc):
    return {
        "cn_w1": nc.dram_tensor("cn_w1", [D, D], F32, kind="ExternalInput").ap(),
        "cn_b1": nc.dram_tensor("cn_b1", [D], F32, kind="ExternalInput").ap(),
        "cn_w2": nc.dram_tensor("cn_w2", [D, 512], F32, kind="ExternalInput").ap(),
        "cn_b2": nc.dram_tensor("cn_b2", [512], F32, kind="ExternalInput").ap(),
        "cn_w3": nc.dram_tensor("cn_w3", [512, 256], F32, kind="ExternalInput").ap(),
        "cn_b3": nc.dram_tensor("cn_b3", [256], F32, kind="ExternalInput").ap(),
        "cn_w4": nc.dram_tensor("cn_w4", [256, 1], F32, kind="ExternalInput").ap(),
        "cn_b4": nc.dram_tensor("cn_b4", [1], F32, kind="ExternalInput").ap(),
        "gc_w1": nc.dram_tensor("gc_w1", [D, 512], F32, kind="ExternalInput").ap(),
        "gc_b1": nc.dram_tensor("gc_b1", [512], F32, kind="ExternalInput").ap(),
        "gc_w2": nc.dram_tensor("gc_w2", [512, 256], F32, kind="ExternalInput").ap(),
        "gc_b2": nc.dram_tensor("gc_b2", [256], F32, kind="ExternalInput").ap(),
        "gc_w3": nc.dram_tensor("gc_w3", [256, 1], F32, kind="ExternalInput").ap(),
        "gc_b3": nc.dram_tensor("gc_b3", [1], F32, kind="ExternalInput").ap(),
        "ar": nc.dram_tensor("ar", [1], F32, kind="ExternalInput").ap(),
    }


def _load_consts(nc, block, es, sems):
    ones128 = nc.inline_tensor(np.ones((128, 128), dtype=np.float32), "ones128")
    onesrow = nc.inline_tensor(np.ones((1, 128), dtype=np.float32), "onesrow")
    iden = nc.inline_tensor(np.eye(128, dtype=np.float32), "iden128")
    ab = np.zeros((128, 2), dtype=np.float32)
    ab[0:64, 0] = 1.0
    ab[64:128, 1] = 1.0
    onesab = nc.inline_tensor(ab, "onesAB")
    rsela = np.zeros((1, 128), dtype=np.float32)
    rsela[0, 0:64] = 1.0
    rselb = np.zeros((1, 128), dtype=np.float32)
    rselb[0, 64:128] = 1.0
    rsa = nc.inline_tensor(rsela, "rsela")
    rsb = nc.inline_tensor(rselb, "rselb")
    h2sel = np.zeros((2, 2, 128), dtype=np.float32)
    h2sel[0, 0, :] = 1.0
    h2sel[1, 1, :] = 1.0
    hselm = np.zeros((128, 16), dtype=np.float32)
    for _h in range(16):
        hselm[_h * 8:(_h + 1) * 8, _h] = 1.0
    rsabm = np.zeros((2, 128), dtype=np.float32)
    rsabm[0, 0:64] = 1.0
    rsabm[1, 64:128] = 1.0
    names = {
        "ones128_sb": es.enter_context(nc.sbuf_tensor("ones128_sb", [128, 128], F32)),
        "onesrow_sb": es.enter_context(nc.sbuf_tensor("onesrow_sb", [1, 128], F32)),
        "iden_sb": es.enter_context(nc.sbuf_tensor("iden_sb", [128, 128], F32)),
        "onesab_sb": es.enter_context(nc.sbuf_tensor("onesab_sb", [128, 2], F32)),
        "rsa_sb": es.enter_context(nc.sbuf_tensor("rsa_sb", [1, 128], F32)),
        "rsb_sb": es.enter_context(nc.sbuf_tensor("rsb_sb", [1, 128], F32)),
        "h2sel_sb": es.enter_context(nc.sbuf_tensor("h2sel_sb", [2, 2, 128], F32)),
        "hsel_sb": es.enter_context(nc.sbuf_tensor("hsel_sb", [128, 16], F32)),
        "rsab_sb": es.enter_context(nc.sbuf_tensor("rsab_sb", [2, 128], F32)),
    }
    c_c = sems["c"]

    @block.sync
    def _(s):
        for sb, src in (("ones128_sb", ones128), ("onesrow_sb", onesrow),
                        ("iden_sb", iden), ("onesab_sb", onesab), ("rsa_sb", rsa),
                        ("rsb_sb", rsb), ("h2sel_sb", nc.inline_tensor(h2sel, "h2sel")),
                        ("hsel_sb", nc.inline_tensor(hselm, "hselm")),
                        ("rsab_sb", nc.inline_tensor(rsabm, "rsabm"))):
            s.dma_start(out=names[sb][...], in_=src.ap()[...]).then_inc(c_c.sem, 16)
    c_c.inc(9 * 16)
    cv = c_c.v

    @block.tensor
    def _(t):
        t.wait_ge(c_c.sem, cv)

    @block.scalar
    def _(s):
        s.wait_ge(c_c.sem, cv)

    @block.vector
    def _(v):
        v.wait_ge(c_c.sem, cv)

    @block.gpsimd
    def _(g):
        g.wait_ge(c_c.sem, cv)
    return names


def _emit_stats(nc, block, es, names, x_sb, st, sems, gate_sem, gate_val):
    t_c, s_c, v_c = sems["t"], sems["s"], sems["v"]
    ones128 = names["ones128_sb"]
    ps_sum, ps_sq, xsq, a_sb, b_sb = (st["ps_sum"], st["ps_sq"], st["xsq"],
                                      st["a_sb"], st["b_sb"])
    prev_mm = st["prev"].get("mm", 0)
    prev_var = st["prev"].get("var", 0)

    @block.scalar
    def _(s):
        s.wait_ge(gate_sem, gate_val)
        s.wait_ge(t_c.sem, prev_mm)
        for c in range(NC_D):
            _last = s.activation(out=xsq[:, c, :], in_=x_sb[:, c, :], func=AF.Square)
        _last.then_inc(s_c.sem, 1)
    sq_done = s_c.inc()

    @block.tensor
    def _(t):
        t.wait_ge(gate_sem, gate_val)
        t.wait_ge(v_c.sem, prev_var)
        for c in range(NC_D):
            t.matmul(ps_sum[:, :], ones128[:, :], x_sb[:, c, :],
                     start=(c == 0), stop=(c == NC_D - 1))
        t.wait_ge(s_c.sem, sq_done)
        for c in range(NC_D):
            _last = t.matmul(ps_sq[:, :], ones128[:, :], xsq[:, c, :],
                     start=(c == 0), stop=(c == NC_D - 1))
        _last.then_inc(t_c.sem, 1)
    mm_done = t_c.inc()

    @block.vector
    def _(v):
        v.wait_ge(t_c.sem, mm_done)
        v.tensor_scalar(out=b_sb[:, :], in0=ps_sum[:, :], scalar1=1.0 / D,
                        scalar2=None, op0=ALU.mult)
        v.drain()
        v.tensor_mul(out=a_sb[:, :], in0=b_sb[:, :], in1=b_sb[:, :])
        v.drain()
        _last = v.scalar_tensor_tensor(out=a_sb[:, :], in0=ps_sq[:, :], scalar=1.0 / D,
                               in1=a_sb[:, :], op0=ALU.mult, op1=ALU.subtract)
        _last.then_inc(v_c.sem, 1)
    var_done = v_c.inc()

    @block.scalar
    def _(s):
        s.wait_ge(v_c.sem, var_done)
        s.activation(out=a_sb[:, :], in_=a_sb[:, :], func=AF.Ln,
                     bias=st["eps"][:, 0:1])
        s.drain()
        s.activation(out=a_sb[:, :], in_=a_sb[:, :], func=AF.Exp, scale=-0.5).then_inc(s_c.sem, 1)
    rs_done = s_c.inc()

    @block.vector
    def _(v):
        v.wait_ge(s_c.sem, rs_done)
        _last = v.scalar_tensor_tensor(out=b_sb[:, :], in0=b_sb[:, :], scalar=-1.0,
                               in1=a_sb[:, :], op0=ALU.mult, op1=ALU.mult)
        _last.then_inc(v_c.sem, 1)
    ab_ready = v_c.inc()
    st["prev"] = {"mm": mm_done, "var": var_done}
    return ab_ready


def _emit_ln_apply(block, x_sb, ln_sb, a_sb, b_sb, sems, ab_ready):
    v_c = sems["v"]

    @block.vector
    def _(v):
        v.wait_ge(v_c.sem, ab_ready)
        for c in range(NC_D):
            v.tensor_mul(out=ln_sb[:, c, :], in0=x_sb[:, c, :], in1=a_sb[:, :])
        v.drain()
        for c in range(NC_D):
            _last = v.tensor_add(out=ln_sb[:, c, :], in0=ln_sb[:, c, :], in1=b_sb[:, :])
        _last.then_inc(v_c.sem, 1)
    return v_c.inc()


def _ffn_weight_loads(nc, block, es, hbm, sems, fence=None):
    d_c = sems["d"]
    w1 = es.enter_context(nc.sbuf_tensor("w1_sb", [128, NC_D, D], F32))
    w2 = es.enter_context(nc.sbuf_tensor("w2_sb", [128, NC_D, 512], F32))
    w3 = es.enter_context(nc.sbuf_tensor("w3_sb", [128, 4, 256], F32))
    w4 = es.enter_context(nc.sbuf_tensor("w4_sb", [128, 2, 1], F32))
    b1c = es.enter_context(nc.sbuf_tensor("b1_sb", [128, NC_D], F32))
    b2c = es.enter_context(nc.sbuf_tensor("b2_sb", [128, 4], F32))
    b3c = es.enter_context(nc.sbuf_tensor("b3_sb", [128, 2], F32))
    b4c = es.enter_context(nc.sbuf_tensor("b4_sb", [1, 1], F32))
    g1 = es.enter_context(nc.sbuf_tensor("g1_sb", [128, NC_D, 512], F32))
    g2 = es.enter_context(nc.sbuf_tensor("g2_sb", [128, 4, 256], F32))
    g3 = es.enter_context(nc.sbuf_tensor("g3_sb", [128, 2, 1], F32))
    gb1 = es.enter_context(nc.sbuf_tensor("gb1_sb", [128, 4], F32))
    gb2 = es.enter_context(nc.sbuf_tensor("gb2_sb", [128, 2], F32))
    gb3 = es.enter_context(nc.sbuf_tensor("gb3_sb", [1, 1], F32))
    arsb = es.enter_context(nc.sbuf_tensor("ar_sb", [1, 1], F32))

    @block.sync
    def _(s):
        if fence is not None:
            s.wait_ge(sems["t"].sem, fence[0])
            s.wait_ge(sems["s"].sem, fence[1])
            s.wait_ge(sems["v"].sem, fence[2])
            s.wait_ge(sems["g"].sem, fence[3])
        for kc in range(NC_D):
            s.dma_start(out=w1[:, kc, :], in_=hbm["cn_w1"][kc * 128:(kc + 1) * 128, :]).then_inc(d_c.sem, 16)
            s.dma_start(out=w2[:, kc, :], in_=hbm["cn_w2"][kc * 128:(kc + 1) * 128, :]).then_inc(d_c.sem, 16)
            s.dma_start(out=g1[:, kc, :], in_=hbm["gc_w1"][kc * 128:(kc + 1) * 128, :]).then_inc(d_c.sem, 16)
        for kc in range(4):
            s.dma_start(out=w3[:, kc, :], in_=hbm["cn_w3"][kc * 128:(kc + 1) * 128, :]).then_inc(d_c.sem, 16)
            s.dma_start(out=g2[:, kc, :], in_=hbm["gc_w2"][kc * 128:(kc + 1) * 128, :]).then_inc(d_c.sem, 16)
        for kc in range(2):
            s.dma_start(out=w4[:, kc, :], in_=hbm["cn_w4"][kc * 128:(kc + 1) * 128, :]).then_inc(d_c.sem, 16)
            s.dma_start(out=g3[:, kc, :], in_=hbm["gc_w3"][kc * 128:(kc + 1) * 128, :]).then_inc(d_c.sem, 16)
        s.dma_start(out=b1c[:, :], in_=hbm["cn_b1"].rearrange("(c p) -> p c", p=128), allow_slow_non_contiguous=True).then_inc(d_c.sem, 16)
        s.dma_start(out=b2c[:, :], in_=hbm["cn_b2"].rearrange("(c p) -> p c", p=128), allow_slow_non_contiguous=True).then_inc(d_c.sem, 16)
        s.dma_start(out=b3c[:, :], in_=hbm["cn_b3"].rearrange("(c p) -> p c", p=128), allow_slow_non_contiguous=True).then_inc(d_c.sem, 16)
        s.dma_start(out=b4c[0:1, 0:1], in_=hbm["cn_b4"][None, :]).then_inc(d_c.sem, 16)
        s.dma_start(out=gb1[:, :], in_=hbm["gc_b1"].rearrange("(c p) -> p c", p=128), allow_slow_non_contiguous=True).then_inc(d_c.sem, 16)
        s.dma_start(out=gb2[:, :], in_=hbm["gc_b2"].rearrange("(c p) -> p c", p=128), allow_slow_non_contiguous=True).then_inc(d_c.sem, 16)
        s.dma_start(out=gb3[0:1, 0:1], in_=hbm["gc_b3"][None, :]).then_inc(d_c.sem, 16)
        s.dma_start(out=arsb[0:1, 0:1], in_=hbm["ar"][None, :]).then_inc(d_c.sem, 16)
    d_c.inc(16 * (3 * NC_D + 2 * 4 + 2 * 2 + 8))
    return ((w1, w2, w3, w4), (b1c, b2c, b3c, b4c), (g1, g2, g3),
            (gb1, gb2, gb3), arsb, d_c.v)


def _emit_gcnet(nc, block, es, names, gvec, gw_sb, gb_sb, gf_out, psg, sems,
                g_ready, denom_tokens, tag, wload=0):
    t_c, s_c, v_c = sems["t"], sems["s"], sems["v"]
    g1, g2, g3 = gw_sb
    gb1, gb2, gb3 = gb_sb
    h1 = es.enter_context(nc.sbuf_tensor(f"gch1{tag}", [128, 4], F32))
    h2 = es.enter_context(nc.sbuf_tensor(f"gch2{tag}", [128, 2], F32))
    gmean = es.enter_context(nc.sbuf_tensor(f"gcm{tag}", [128, NC_D], F32))

    @block.vector
    def _(v):
        v.wait_ge(v_c.sem, g_ready)
        _last = v.tensor_scalar(out=gmean[:, :], in0=gvec[:, :], scalar1=1.0 / denom_tokens,
                        scalar2=None, op0=ALU.mult)
        _last.then_inc(v_c.sem, 1)
    gm_done = v_c.inc()

    @block.tensor
    def _(t):
        t.wait_ge(v_c.sem, gm_done)
        t.wait_ge(sems["d"].sem, wload)
        for mc in range(4):
            for kc in range(NC_D):
                _last = t.matmul(psg[:, mc:mc + 1], g1[:, kc, mc * 128:(mc + 1) * 128],
                         gmean[:, kc:kc + 1], start=(kc == 0), stop=(kc == NC_D - 1))
        _last.then_inc(t_c.sem, 1)
    mm1 = t_c.inc()

    @block.scalar
    def _(s):
        s.wait_ge(t_c.sem, mm1)
        for mc in range(4):
            _last = s.activation(out=h1[:, mc:mc + 1], in_=psg[:, mc:mc + 1], func=AF.Gelu,
                         bias=gb1[:, mc:mc + 1])
        _last.then_inc(s_c.sem, 1)
    a1 = s_c.inc()

    @block.tensor
    def _(t):
        t.wait_ge(s_c.sem, a1)
        for mc in range(2):
            for kc in range(4):
                _last = t.matmul(psg[:, mc:mc + 1], g2[:, kc, mc * 128:(mc + 1) * 128],
                         h1[:, kc:kc + 1], start=(kc == 0), stop=(kc == 3))
        _last.then_inc(t_c.sem, 1)
    mm2 = t_c.inc()

    @block.scalar
    def _(s):
        s.wait_ge(t_c.sem, mm2)
        for mc in range(2):
            _last = s.activation(out=h2[:, mc:mc + 1], in_=psg[:, mc:mc + 1], func=AF.Gelu,
                         bias=gb2[:, mc:mc + 1])
        _last.then_inc(s_c.sem, 1)
    a2 = s_c.inc()

    @block.tensor
    def _(t):
        t.wait_ge(s_c.sem, a2)
        for kc in range(2):
            _last = t.matmul(psg[0:1, 3:4], g3[:, kc, 0:1], h2[:, kc:kc + 1],
                     start=(kc == 0), stop=(kc == 1))
        _last.then_inc(t_c.sem, 1)
    mm3 = t_c.inc()

    @block.scalar
    def _(s):
        s.wait_ge(t_c.sem, mm3)
        _last = s.activation(out=gf_out[0:1, 0:1], in_=psg[0:1, 3:4], func=AF.Sigmoid,
                     bias=gb3[0:1, 0:1])
        _last.then_inc(s_c.sem, 1)
    return s_c.inc()


def _emit_ab_scalars(block, gf, arsb, AB_sb, sems, gf_done, wload):
    s_c, v_c, d_c = sems["s"], sems["v"], sems["d"]

    @block.vector
    def _(v):
        v.wait_ge(s_c.sem, gf_done)
        v.wait_ge(d_c.sem, wload)
        v.tensor_scalar(out=AB_sb[0:1, 0:1], in0=gf[0:1, 0:1], scalar1=0.7,
                        scalar2=-0.5, op0=ALU.mult, op1=ALU.add)
        v.drain()
        v.tensor_mul(out=AB_sb[0:1, 0:1], in0=AB_sb[0:1, 0:1], in1=arsb[0:1, 0:1])
        v.drain()
        v.tensor_scalar(out=AB_sb[0:1, 0:1], in0=AB_sb[0:1, 0:1], scalar1=2.0,
                        scalar2=0.1, op0=ALU.mult, op1=ALU.add)
        v.tensor_scalar(out=AB_sb[0:1, 1:2], in0=arsb[0:1, 0:1], scalar1=0.6,
                        scalar2=None, op0=ALU.mult)
        v.drain()
        v.drain().then_inc(v_c.sem, 1)
    return v_c.inc()


def _emit_ffn_cycles(nc, block, names, w_sb, b_sb, bufs, h1, h2, h3,
                     ps_mm, ps_bc, tf_row, c_row, AB_sb, sems, in_ready, wload=0):
    t_c, s_c, v_c = sems["t"], sems["s"], sems["v"]
    onesrow = names["onesrow_sb"]
    w1, w2, w3, w4 = w_sb
    b1c, b2c, b3c, b4c = b_sb
    wv = in_ready
    for cyc in range(3):
        cur = bufs[cyc % 2]
        nxt = bufs[(cyc + 1) % 2]

        for (wt, bt, hsrc, hdst, n_m, n_k) in (
                (w1, b1c, cur, h1, NC_D, NC_D),
                (w2, b2c, h1, h2, 4, NC_D),
                (w3, b3c, h2, h3, 2, 4)):
            act_base = s_c.v
            t_base = t_c.v

            @block.tensor
            def _(t, wt=wt, hsrc=hsrc, n_m=n_m, n_k=n_k, wv=wv,
                  act_base=act_base, first=(hsrc is cur), cyc=cyc):
                if first:
                    t.wait_ge(v_c.sem, wv)
                    if cyc == 0:
                        t.wait_ge(sems["d"].sem, wload)
                else:
                    t.wait_ge(s_c.sem, act_base)
                for mc in range(n_m):
                    if mc >= 2:
                        t.wait_ge(s_c.sem, act_base + mc - 1)
                    for kc in range(n_k):
                        _last = t.matmul(ps_mm[:, mc % 2, :], wt[:, kc, mc * 128:(mc + 1) * 128],
                                 hsrc[:, kc, :], start=(kc == 0), stop=(kc == n_k - 1))
                    _last.then_inc(t_c.sem, 1)
            t_c.inc(n_m)

            @block.scalar
            def _(s, bt=bt, hdst=hdst, n_m=n_m, t_base=t_base):
                for mc in range(n_m):
                    s.wait_ge(t_c.sem, t_base + mc + 1)
                    s.activation(out=hdst[:, mc, :], in_=ps_mm[:, mc % 2, :],
                                 func=AF.Gelu, bias=bt[:, mc:mc + 1]
                                 ).then_inc(s_c.sem, 1)
            s_c.inc(n_m)
        h3_done = s_c.v

        @block.tensor
        def _(t, h3_done=h3_done):
            t.wait_ge(s_c.sem, h3_done)
            for kc in range(2):
                _last = t.matmul(ps_mm[0:1, 0, :], w4[:, kc, 0:1], h3[:, kc, :],
                         start=(kc == 0), stop=(kc == 1))
            _last.then_inc(t_c.sem, 1)
        z_done = t_c.inc()

        @block.scalar
        def _(s, z_done=z_done):
            s.wait_ge(t_c.sem, z_done)
            s.activation(out=tf_row[0:1, :], in_=ps_mm[0:1, 0, :], func=AF.Sigmoid,
                         bias=b4c[0:1, 0:1]).then_inc(s_c.sem, 1)
        tf_done = s_c.inc()

        @block.vector
        def _(v, tf_done=tf_done):
            v.wait_ge(s_c.sem, tf_done)
            _last = v.tensor_scalar(out=c_row[0:1, :], in0=tf_row[0:1, :],
                            scalar1=AB_sb[0:1, 1:2], scalar2=AB_sb[0:1, 0:1],
                            op0=ALU.mult, op1=ALU.add)
            _last.then_inc(v_c.sem, 1)
        c_done = v_c.inc()

        @block.tensor
        def _(t, c_done=c_done):
            t.wait_ge(v_c.sem, c_done)
            _last = t.matmul(ps_bc[:, :], onesrow[0:1, :], c_row[0:1, :],
                     start=True, stop=True)
            _last.then_inc(t_c.sem, 1)
        bc_done = t_c.inc()

        @block.vector
        def _(v, cur=cur, nxt=nxt, bc_done=bc_done):
            v.wait_ge(t_c.sem, bc_done)
            sig = [(4, 1.0), (5, 1.0), (6, -1.0), (7, -1.0),
                   (0, -1.0), (1, -1.0), (2, 1.0), (3, 1.0)]
            for c in range(NC_D):
                srcc, sgn = sig[c]
                if sgn > 0:
                    v.tensor_mul(out=nxt[:, c, :], in0=cur[:, srcc, :], in1=ps_bc[:, :])
                else:
                    v.scalar_tensor_tensor(out=nxt[:, c, :], in0=cur[:, srcc, :],
                                           scalar=-1.0, in1=ps_bc[:, :],
                                           op0=ALU.mult, op1=ALU.mult)
            v.drain()
            for c in range(NC_D):
                _last = v.tensor_add(out=nxt[:, c, :], in0=nxt[:, c, :], in1=cur[:, c, :])
            _last.then_inc(v_c.sem, 1)
        wv = v_c.inc()
    return bufs[1], wv


def _emit_ffn_block_sweep(nc, block, es, names, w_sb, b_sbw, AB_sb, st, sems,
                          src_ap, out_ap, half_coef, src_gate, tag, fence=None,
                          wload=0):
    t_c, s_c, v_c, d_c, g_c = (sems[k] for k in ("t", "s", "v", "d", "g"))
    xblk = es.enter_context(nc.sbuf_tensor(f"fxb{tag}", [128, 2, NC_D, BLK], F32))
    outa = es.enter_context(nc.sbuf_tensor(f"fa{tag}", [128, NC_D, BLK], F32))
    outb = es.enter_context(nc.sbuf_tensor(f"fb{tag}", [128, NC_D, BLK], F32))
    h1 = es.enter_context(nc.sbuf_tensor(f"fh1{tag}", [128, NC_D, BLK], F32))
    h2 = es.enter_context(nc.sbuf_tensor(f"fh2{tag}", [128, 4, BLK], F32))
    h3 = es.enter_context(nc.sbuf_tensor(f"fh3{tag}", [128, 2, BLK], F32))
    tf_row = es.enter_context(nc.sbuf_tensor(f"ftf{tag}", [1, BLK], F32))
    c_row = es.enter_context(nc.sbuf_tensor(f"fc{tag}", [1, BLK], F32))
    ps_mm = es.enter_context(nc.psum_tensor(f"fpsm{tag}", [128, 2, BLK], F32))
    ps_bc = es.enter_context(nc.psum_tensor(f"fpsb{tag}", [128, BLK], F32))
    ofree = {}
    ystore = {}
    for ob in range(NBLK_O):
        buf = ob % 2

        x_c = sems[f"x{buf}"]

        @block.sync
        def _(s, ob=ob, buf=buf):
            if ob == 0 and fence is not None:
                s.wait_ge(t_c.sem, fence[0])
                s.wait_ge(s_c.sem, fence[1])
                s.wait_ge(v_c.sem, fence[2])
            if src_gate is not None:
                s.wait_ge(src_gate[0], src_gate[1])
            if ob >= 2:
                s.wait_ge(v_c.sem, ofree[ob - 2])
            for c in range(NC_D):
                s.dma_start(out=xblk[:, buf, c, :],
                            in_=src_ap[c * 128:(c + 1) * 128, ob * BLK:(ob + 1) * BLK]
                            ).then_inc(x_c.sem, 16)
        x_c.inc(16 * NC_D)

        ab_ready = _emit_stats(nc, block, es, names, xblk[:, buf], st, sems,
                               x_c.sem, x_c.v)
        if ob >= 1:
            @block.vector
            def _(v, ob=ob):
                v.wait_ge(g_c.sem, ystore[ob - 1])
        ln_done = _emit_ln_apply(block, xblk[:, buf], outa, st["a_sb"], st["b_sb"],
                                 sems, ab_ready)

        final_buf, fin_wv = _emit_ffn_cycles(
            nc, block, names, w_sb, b_sbw, (outa, outb), h1, h2, h3,
            ps_mm, ps_bc, tf_row, c_row, AB_sb, sems, ln_done, wload=wload)
        other_buf = outa if final_buf is outb else outb

        @block.vector
        def _(v, buf=buf, final_buf=final_buf, other_buf=other_buf, fin_wv=fin_wv):
            v.wait_ge(v_c.sem, fin_wv)
            for c in range(NC_D):
                _last = v.scalar_tensor_tensor(out=other_buf[:, c, :], in0=final_buf[:, c, :],
                                       scalar=half_coef, in1=xblk[:, buf, c, :],
                                       op0=ALU.mult, op1=ALU.add)
            _last.then_inc(v_c.sem, 1)
        y_done = v_c.inc()
        ofree[ob] = y_done

        @block.gpsimd
        def _(g, ob=ob, other_buf=other_buf, y_done=y_done):
            g.wait_ge(v_c.sem, y_done)
            for c in range(NC_D):
                g.dma_start(out=out_ap[c * 128:(c + 1) * 128, ob * BLK:(ob + 1) * BLK],
                            in_=other_buf[:, c, :]).then_inc(g_c.sem, 16)
        g_c.inc(16 * NC_D)
        ystore[ob] = g_c.v
    return g_c.v



def build_l1():
    nc = bass.Bass("TRN2", target_bir_lowering=False, debug=False)
    x_all = nc.dram_tensor("x_all", [D, T], F32, kind="ExternalInput").ap()
    x_own = nc.dram_tensor("x_own", [D, OWN], F32, kind="ExternalInput").ap()
    hbm = _declare_ffn_hbm(nc)
    x1_out = nc.dram_tensor("x1_own", [D, OWN], F32, kind="ExternalOutput").ap()

    es = contextlib.ExitStack()
    with es:
        sems = _mk_sems(nc, es)
        t_c, s_c, v_c, d_c, g_c = (sems[k] for k in ("t", "s", "v", "d", "g"))
        block = es.enter_context(nc.Block())
        names = _load_consts(nc, block, es, sems)
        (w_sb, b_sbw, gw_sb, gb_sb, arsb, wload) = _ffn_weight_loads(
            nc, block, es, hbm, sems)

        st = {
            "ps_sum": es.enter_context(nc.psum_tensor("ps_sum", [128, BLK], F32)),
            "ps_sq": es.enter_context(nc.psum_tensor("ps_sq", [128, BLK], F32)),
            "xsq": es.enter_context(nc.sbuf_tensor("xsq", [128, NC_D, BLK], F32)),
            "a_sb": es.enter_context(nc.sbuf_tensor("a_sb", [128, BLK], F32)),
            "b_sb": es.enter_context(nc.sbuf_tensor("b_sb", [128, BLK], F32)),
            "eps": es.enter_context(nc.sbuf_tensor("eps_col", [128, 1], F32)),
            "prev": {},
        }
        psg = es.enter_context(nc.psum_tensor("ps_gc", [128, 4], F32))
        gacc = es.enter_context(nc.sbuf_tensor("gacc", [128, NC_D], F32))
        rtmp = es.enter_context(nc.sbuf_tensor("rtmp", [128, NC_D], F32))
        gf = es.enter_context(nc.sbuf_tensor("gf", [1, 1], F32))
        AB_sb = es.enter_context(nc.sbuf_tensor("AB", [1, 2], F32))

        @block.vector
        def _(v):
            v.memset(gacc[:, :], 0.0)
            v.memset(st["eps"][:, :], EPS).then_inc(v_c.sem, 1)
        v_c.inc()

        with contextlib.ExitStack() as es2:
            xblk = es2.enter_context(nc.sbuf_tensor("sxb", [128, 2, NC_D, BLK], F32))
            lnt = es2.enter_context(nc.sbuf_tensor("sln", [128, NC_D, BLK], F32))
            xfree = {}
            for tb in range(NBLK_B):
                buf = tb % 2

                x_c = sems[f"x{buf}"]

                @block.sync
                def _(s, tb=tb, buf=buf):
                    if tb >= 2:
                        s.wait_ge(v_c.sem, xfree[tb - 2])
                    for c in range(NC_D):
                        s.dma_start(out=xblk[:, buf, c, :],
                                    in_=x_all[c * 128:(c + 1) * 128, tb * BLK:(tb + 1) * BLK]
                                    ).then_inc(x_c.sem, 16)
                x_c.inc(16 * NC_D)

                ab_ready = _emit_stats(nc, block, es, names, xblk[:, buf], st,
                                       sems, x_c.sem, x_c.v)
                ln_done = _emit_ln_apply(block, xblk[:, buf], lnt, st["a_sb"],
                                         st["b_sb"], sems, ab_ready)

                @block.vector
                def _(v, ln_done=ln_done):
                    v.wait_ge(v_c.sem, ln_done)
                    for c in range(NC_D):
                        v.reduce_sum(out=rtmp[:, c:c + 1], in_=lnt[:, c, :], axis=AX.X)
                    v.drain()
                    v.tensor_add(out=gacc[:, :], in0=gacc[:, :], in1=rtmp[:, :]).then_inc(v_c.sem, 1)
                xfree[tb] = v_c.inc()

        gf_done = _emit_gcnet(nc, block, es, names, gacc, gw_sb, gb_sb, gf, psg,
                              sems, xfree[NBLK_B - 1], float(T), "1", wload=wload)
        _emit_ab_scalars(block, gf, arsb, AB_sb, sems, gf_done, wload)
        fence1 = (t_c.v, s_c.v, v_c.v)

        gfin = _emit_ffn_block_sweep(nc, block, es, names, w_sb, b_sbw, AB_sb, st,
                                     sems, x_own, x1_out, 0.5, None, "1",
                                     fence=fence1, wload=wload)

        @block.gpsimd
        def _(g):
            g.wait_ge(g_c.sem, gfin)
    return nc



def build_l2():
    nc = bass.Bass("TRN2", target_bir_lowering=False, debug=False)
    x_all = nc.dram_tensor("x_all", [D, T], F32, kind="ExternalInput").ap()
    x_own = nc.dram_tensor("x_own", [D, OWN], F32, kind="ExternalInput").ap()
    wq = nc.dram_tensor("wq", [D, D], F32, kind="ExternalInput").ap()
    wk = nc.dram_tensor("wk", [D, D], F32, kind="ExternalInput").ap()
    wv = nc.dram_tensor("wv", [D, D], F32, kind="ExternalInput").ap()
    bq = nc.dram_tensor("bq", [D], F32, kind="ExternalInput").ap()
    bk = nc.dram_tensor("bk", [D], F32, kind="ExternalInput").ap()
    bv = nc.dram_tensor("bv", [D], F32, kind="ExternalInput").ap()
    otp_out = nc.dram_tensor("otp", [NP, 128, OWN], F32, kind="ExternalOutput").ap()
    l_out = nc.dram_tensor("l_out", [H, OWN], F32, kind="ExternalOutput").ap()
    nsq_out = nc.dram_tensor("nsq_out", [H, OWN], F32, kind="ExternalOutput").ap()
    kfmh = nc.dram_tensor("kfmh_scr", [NP, 128, T], BF16, kind="Internal").ap()
    kfml = nc.dram_tensor("kfml_scr", [NP, 128, T], BF16, kind="Internal").ap()
    vtm = nc.dram_tensor("vtm_scr", [T, D], F32, kind="Internal").ap()
    qfmh = nc.dram_tensor("qfmh_scr", [NP, 128, OWN], BF16, kind="Internal").ap()
    qfml = nc.dram_tensor("qfml_scr", [NP, 128, OWN], BF16, kind="Internal").ap()

    es = contextlib.ExitStack()
    with es:
        sems = _mk_sems(nc, es)
        t_c, s_c, v_c, d_c, g_c = (sems[k] for k in ("t", "s", "v", "d", "g"))
        block = es.enter_context(nc.Block())
        names = _load_consts(nc, block, es, sems)

        kmax2 = es.enter_context(nc.sbuf_tensor("kmax2", [2, NP], F32))
        qmax2 = es.enter_context(nc.sbuf_tensor("qmax2", [2, NP], F32))
        mval = es.enter_context(nc.sbuf_tensor("mval", [2, NP], F32))
        mcol = es.enter_context(nc.sbuf_tensor("mcol", [128, 2 * NP], F32))

        @block.vector
        def _(v):
            v.memset(kmax2[:, :], 0.0)
            v.memset(qmax2[:, :], 0.0).then_inc(v_c.sem, 1)
        v_c.inc()

        with contextlib.ExitStack() as es2:
            wq_sb = es2.enter_context(nc.sbuf_tensor("wq_sb", [128, NC_D, D], F32))
            wk_sb = es2.enter_context(nc.sbuf_tensor("wk_sb", [128, NC_D, D], F32))
            wv_sb = es2.enter_context(nc.sbuf_tensor("wv_sb", [128, NC_D, D], F32))
            bq_sb = es2.enter_context(nc.sbuf_tensor("bq_sb", [128, NP], F32))
            bk_sb = es2.enter_context(nc.sbuf_tensor("bk_sb", [128, NP], F32))
            bv_sb = es2.enter_context(nc.sbuf_tensor("bv_sb", [1, D], F32))

            @block.sync
            def _(s):
                for kc in range(NC_D):
                    s.dma_start(out=wq_sb[:, kc, :], in_=wq[kc * 128:(kc + 1) * 128, :]).then_inc(d_c.sem, 16)
                    s.dma_start(out=wk_sb[:, kc, :], in_=wk[kc * 128:(kc + 1) * 128, :]).then_inc(d_c.sem, 16)
                    s.dma_start(out=wv_sb[:, kc, :], in_=wv[kc * 128:(kc + 1) * 128, :]).then_inc(d_c.sem, 16)
                s.dma_start(out=bq_sb[:, :], in_=bq.rearrange("(c p) -> p c", p=128), allow_slow_non_contiguous=True).then_inc(d_c.sem, 16)
                s.dma_start(out=bk_sb[:, :], in_=bk.rearrange("(c p) -> p c", p=128), allow_slow_non_contiguous=True).then_inc(d_c.sem, 16)
                s.dma_start(out=bv_sb[0:1, :], in_=bv[None, :]).then_inc(d_c.sem, 16)
            d_c.inc(16 * (3 * NC_D + 3))
            wload = d_c.v

            st = {
                "ps_sum": es2.enter_context(nc.psum_tensor("ps_sum", [128, BLK], F32)),
                "ps_sq": es2.enter_context(nc.psum_tensor("ps_sq", [128, BLK], F32)),
                "xsq": es2.enter_context(nc.sbuf_tensor("xsq", [128, NC_D, BLK], F32)),
                "a_sb": es2.enter_context(nc.sbuf_tensor("a_sb", [128, BLK], F32)),
                "b_sb": es2.enter_context(nc.sbuf_tensor("b_sb", [128, BLK], F32)),
                "eps": es2.enter_context(nc.sbuf_tensor("eps_col", [128, 1], F32)),
                "prev": {},
            }
            @block.vector
            def _(v):
                v.memset(st["eps"][:, :], EPS).then_inc(v_c.sem, 1)
            v_c.inc()

            xblk = es2.enter_context(nc.sbuf_tensor("xblk", [128, 2, NC_D, BLK], F32))
            lnt = es2.enter_context(nc.sbuf_tensor("lnt", [128, NC_D, BLK], F32))
            ktmp = es2.enter_context(nc.sbuf_tensor("ktmp", [128, 2, BLK], F32))
            khit = es2.enter_context(nc.sbuf_tensor("khit", [128, 2, BLK], BF16))
            klot = es2.enter_context(nc.sbuf_tensor("klot", [128, 2, BLK], BF16))
            ksq = es2.enter_context(nc.sbuf_tensor("ksq", [128, 2, BLK], F32))
            vtmp = es2.enter_context(nc.sbuf_tensor("vtmp", [128, 2, BLK], F32))
            km_t = es2.enter_context(nc.sbuf_tensor("km_t", [2, 1], F32))
            ps_kv = es2.enter_context(nc.psum_tensor("ps_kv", [128, 2, BLK], F32))
            ps_n2 = es2.enter_context(nc.psum_tensor("ps_n2", [2, BLK], F32))

            def proj_sweep(src_ap, nblk, do_q, tagp):
                lnfree = {}
                sweep_fence = v_c.v
                for tb in range(nblk):
                    buf = tb % 2

                    x_c = sems[f"x{buf}"]

                    @block.sync
                    def _(s, tb=tb, buf=buf):
                        if tb >= 2:
                            s.wait_ge(v_c.sem, lnfree[tb - 2])
                        else:
                            s.wait_ge(v_c.sem, sweep_fence)
                        for c in range(NC_D):
                            s.dma_start(out=xblk[:, buf, c, :],
                                        in_=src_ap[c * 128:(c + 1) * 128,
                                                   tb * BLK:(tb + 1) * BLK]
                                        ).then_inc(x_c.sem, 16)
                    x_c.inc(16 * NC_D)

                    ab_ready = _emit_stats(nc, block, es, names, xblk[:, buf], st,
                                           sems, x_c.sem, x_c.v)
                    t_prev = t_c.v

                    @block.vector
                    def _(v, buf=buf, ab_ready=ab_ready, t_prev=t_prev):
                        v.wait_ge(v_c.sem, ab_ready)
                        v.wait_ge(t_c.sem, t_prev)
                        for c in range(NC_D):
                            v.tensor_mul(out=lnt[:, c, :], in0=xblk[:, buf, c, :],
                                         in1=st["a_sb"][:, :])
                        v.drain()
                        for c in range(NC_D):
                            _last = v.tensor_add(out=lnt[:, c, :], in0=lnt[:, c, :],
                                         in1=st["b_sb"][:, :])
                        _last.then_inc(v_c.sem, 1)
                    ln_done = v_c.inc()
                    lnfree[tb] = ln_done

                    kst = sems.setdefault("_kst", {})
                    vst = sems.setdefault("_vst", {})
                    wmat = wq_sb if do_q else wk_sb
                    bcol = bq_sb if do_q else bk_sb
                    mx = qmax2 if do_q else kmax2
                    hist = sems.setdefault("_pskv_hist", {})

                    def emit_n2(pend, mx=mx):
                        pp, pev, pn2read = pend

                        @block.tensor
                        def _(t, pp=pp, pev=pev, pn2read=pn2read):
                            t.wait_ge(s_c.sem, pev)
                            t.wait_ge(v_c.sem, pn2read)
                            _last = t.matmul(ps_n2[:, :], names["onesab_sb"][:, :],
                                     ksq[:, pp % 2, :], start=True, stop=True)
                            _last.then_inc(t_c.sem, 1)
                        n2_mm = t_c.inc()

                        @block.vector
                        def _(v, pp=pp, n2_mm=n2_mm, mx=mx):
                            v.wait_ge(t_c.sem, n2_mm)
                            v.reduce_max(out=km_t[:, 0:1], in_=ps_n2[:, :], axis=AX.X)
                            v.drain()
                            _last = v.tensor_max(out=mx[:, pp:pp + 1],
                                         in0=mx[:, pp:pp + 1], in1=km_t[:, 0:1])
                            _last.then_inc(v_c.sem, 1)
                        sems["_n2read"] = v_c.inc()

                    pend = None
                    for p in range(NP):
                        bank_free = hist.get(p % 2)

                        @block.tensor
                        def _(t, p=p, ln_done=ln_done, bank_free=bank_free):
                            t.wait_ge(v_c.sem, ln_done)
                            t.wait_ge(d_c.sem, wload)
                            if bank_free is not None:
                                t.wait_ge(bank_free[0].sem, bank_free[1])
                            for kc in range(NC_D):
                                _last = t.matmul(ps_kv[:, p % 2, :],
                                         wmat[:, kc, p * 128:(p + 1) * 128],
                                         lnt[:, kc, :], start=(kc == 0),
                                         stop=(kc == NC_D - 1))
                            _last.then_inc(t_c.sem, 1)
                        k_mm = t_c.inc()

                        if pend is not None:
                            emit_n2(pend)

                        @block.scalar
                        def _(s, p=p, k_mm=k_mm, kprev=g_c.v):
                            s.wait_ge(t_c.sem, k_mm)
                            s.wait_ge(g_c.sem, kprev)
                            s.activation(out=ktmp[:, p % 2, :], in_=ps_kv[:, p % 2, :],
                                         func=AF.Identity, bias=bcol[:, p:p + 1]
                                         ).then_inc(s_c.sem, 1)
                            s.drain()
                            _last = s.activation(out=ksq[:, p % 2, :],
                                         in_=ktmp[:, p % 2, :], func=AF.Square)
                            _last.then_inc(s_c.sem, 1)
                        s_c.inc(2)
                        k_evac = s_c.v
                        hist[p % 2] = (s_c, k_evac - 1)
                        pend = (p, k_evac, sems.get("_n2read", 0))

                        prev_st = g_c.v

                        @block.vector
                        def _(v, p=p, k_evac=k_evac, prev_st=prev_st):
                            v.wait_ge(s_c.sem, k_evac - 1)
                            v.wait_ge(g_c.sem, prev_st)
                            v.tensor_copy(out=khit[:, p % 2, :], in_=ktmp[:, p % 2, :])
                            v.drain()
                            v.tensor_sub(out=klot[:, p % 2, :], in0=ktmp[:, p % 2, :],
                                         in1=khit[:, p % 2, :]).then_inc(v_c.sem, 1)
                        hl_done = v_c.inc()

                        @block.gpsimd
                        def _(g, p=p, tb=tb, hl_done=hl_done):
                            g.wait_ge(v_c.sem, hl_done)
                            dh, dl = (qfmh, qfml) if do_q else (kfmh, kfml)
                            g.dma_start(out=dh[p, :, tb * BLK:(tb + 1) * BLK],
                                        in_=khit[:, p % 2, :]).then_inc(g_c.sem, 16)
                            g.dma_start(out=dl[p, :, tb * BLK:(tb + 1) * BLK],
                                        in_=klot[:, p % 2, :]).then_inc(g_c.sem, 16)
                        g_c.inc(32)
                        kst[(tb, p)] = g_c.v
                        kst["last"] = g_c.v if p == NP - 1 else kst.get("last", 0)
                    emit_n2(pend)

                    if not do_q:
                        k_all_evac = s_c.v
                        for stile in range(BLK // 128):
                            for hf in range(2):
                                prev_vevac = v_c.v

                                @block.tensor
                                def _(t, stile=stile, hf=hf, ln_done=ln_done,
                                      k_all_evac=k_all_evac, prev_vevac=prev_vevac):
                                    t.wait_ge(v_c.sem, ln_done)
                                    t.wait_ge(s_c.sem, k_all_evac)
                                    t.wait_ge(v_c.sem, prev_vevac)
                                    for kc in range(NC_D):
                                        t.matmul(ps_kv[:, hf, :],
                                                 lnt[:, kc, stile * 128:(stile + 1) * 128],
                                                 wv_sb[:, kc, hf * 512:(hf + 1) * 512],
                                                 start=(kc == 0), stop=False,
                                                 skip_group_check=True)
                                    _last = t.matmul(ps_kv[:, hf, :], names["onesrow_sb"][0:1, :],
                                             bv_sb[0:1, hf * 512:(hf + 1) * 512],
                                             start=False, stop=True,
                                             skip_group_check=True)
                                    _last.then_inc(t_c.sem, 1)
                                v_mm = t_c.inc()

                                @block.vector
                                def _(v, hf=hf, v_mm=v_mm, vprev=g_c.v):
                                    v.wait_ge(t_c.sem, v_mm)
                                    v.wait_ge(g_c.sem, vprev)
                                    v.tensor_copy(out=vtmp[:, hf, :], in_=ps_kv[:, hf, :]).then_inc(v_c.sem, 1)
                                v_evac = v_c.inc()
                                sems.setdefault("_pskv_hist", {})[hf] = (v_c, v_evac)

                                @block.gpsimd
                                def _(g, tb=tb, stile=stile, hf=hf, v_evac=v_evac):
                                    g.wait_ge(v_c.sem, v_evac)
                                    g.dma_start(
                                        out=vtm[tb * BLK + stile * 128:
                                                tb * BLK + (stile + 1) * 128,
                                                hf * 512:(hf + 1) * 512],
                                        in_=vtmp[:, hf, :]).then_inc(g_c.sem, 16)
                                g_c.inc(16)
                                vst[(tb, stile, hf)] = g_c.v
                                if stile == BLK // 128 - 1 and hf == 1:
                                    vst["last"] = g_c.v

            proj_sweep(x_all, NBLK_B, False, "kv")
            proj_sweep(x_own, NBLK_O, True, "q")
            prod_done_g = g_c.v
            prod_done_v = v_c.v
            prod_done_t = t_c.v
            prod_done_s = s_c.v

        es_mc = contextlib.ExitStack()
        ps_misc = es_mc.enter_context(nc.psum_tensor("ps_misc", [128, 16], F32))

        @block.vector
        def _(v):
            v.wait_ge(v_c.sem, prod_done_v)
            v.tensor_mul(out=mval[:, :], in0=qmax2[:, :], in1=kmax2[:, :]).then_inc(v_c.sem, 1)
        mv1 = v_c.inc()

        @block.scalar
        def _(s):
            s.wait_ge(v_c.sem, mv1)
            s.activation(out=mval[:, :], in_=mval[:, :], func=AF.Sqrt).then_inc(s_c.sem, 1)
        mv2 = s_c.inc()

        @block.vector
        def _(v):
            v.wait_ge(s_c.sem, mv2)
            _last = v.tensor_scalar(out=mval[:, :], in0=mval[:, :], scalar1=-SCALE,
                            scalar2=None, op0=ALU.mult)
            _last.then_inc(v_c.sem, 1)
        mv3 = v_c.inc()

        @block.tensor
        def _(t):
            t.wait_ge(v_c.sem, mv3)
            for h2 in range(2):
                _last = t.matmul(ps_misc[:, h2 * NP:(h2 + 1) * NP],
                         names["h2sel_sb"][:, h2, :], mval[:, :],
                         start=True, stop=True)
            _last.then_inc(t_c.sem, 1)
        mc_mm = t_c.inc()

        @block.vector
        def _(v):
            v.wait_ge(t_c.sem, mc_mm)
            _last = v.tensor_copy(
                out=mcol[:, :].rearrange("p (pp hh) -> p hh pp", hh=2),
                in_=ps_misc[:, :].rearrange("p (hh pp) -> p hh pp", hh=2))
            _last.then_inc(v_c.sem, 1)
        mcol_done = v_c.inc()

        es_mc.close()
        ktile = es.enter_context(nc.sbuf_tensor("ktile", [128, 2, 2, 128], BF16))
        vt = es.enter_context(nc.sbuf_tensor("vt", [128, 2, NKC, 132], F32))
        qth = es.enter_context(nc.sbuf_tensor("qth", [128, 2, 2, OWN], BF16))
        qtl = es.enter_context(nc.sbuf_tensor("qtl", [64, 2, 2, OWN], BF16))
        pt = es.enter_context(nc.sbuf_tensor("pt", [128, 2, 2, QGRP], F32))
        otA = es.enter_context(nc.sbuf_tensor("otA", [64, OWN], F32))
        otB = es.enter_context(nc.sbuf_tensor("otB", [64, OWN], F32))
        ot_h = [otA, otB]
        lsb = es.enter_context(nc.sbuf_tensor("lsb", [65, 2, OWN], F32))
        sqsb = es.enter_context(nc.sbuf_tensor("sqsb", [64, 2, QGRP], F32))
        nsqsb = es.enter_context(nc.sbuf_tensor("nsqsb", [1, 2, OWN], F32))
        ps_sc = es.enter_context(nc.psum_tensor("ps_sc", [128, 2, 2, QGRP], F32))
        ps_po = es.enter_context(nc.psum_tensor("ps_po", [65, 2, QGRP], F32))
        ps_nsq = es.enter_context(nc.psum_tensor("ps_nsq", [1, QGRP], F32))

        @block.vector
        def _(v):
            v.memset(vt[:, :, :, 64:65].rearrange("p b k o -> p (b k o)"), 1.0)
            v.memset(vt[:, :, :, 129:130].rearrange("p b k o -> p (b k o)"), 1.0).then_inc(v_c.sem, 1)
        ones_done = v_c.inc()

        pair_free = {}
        ot_store = {}

        def emit_nsq(pend):
            pqg, sqd = pend
            for h2 in range(2):
                prev_cp = v_c.v

                @block.tensor
                def _(t, h2=h2, sqd=sqd, prev_cp=prev_cp):
                    t.wait_ge(s_c.sem, sqd[h2])
                    t.wait_ge(v_c.sem, prev_cp)
                    t.matmul(ps_nsq[0:1, :], names["ones128_sb"][0:64, 0:1],
                             sqsb[:, h2, :], start=True, stop=True).then_inc(t_c.sem, 1)
                nsq_mm = t_c.inc()

                @block.vector
                def _(v, h2=h2, pqg=pqg, nsq_mm=nsq_mm):
                    v.wait_ge(t_c.sem, nsq_mm)
                    v.tensor_copy(out=nsqsb[0:1, h2, pqg * QGRP:(pqg + 1) * QGRP],
                                  in_=ps_nsq[0:1, :]).then_inc(v_c.sem, 1)
                v_c.inc()

        for p in range(NP):
            pb = p % 2
            pending_nsq = None

            p_c = sems[f"p{pb}"]

            @block.sync
            def _(s, p=p, pb=pb):
                s.wait_ge(g_c.sem, prod_done_g)
                if p == 0:
                    s.wait_ge(t_c.sem, prod_done_t)
                    s.wait_ge(s_c.sem, prod_done_s)
                    s.wait_ge(v_c.sem, prod_done_v)
                if p >= 2:
                    s.wait_ge(t_c.sem, pair_free[p - 2])
                for h2 in range(2):
                    s.dma_start(out=qth[0:64, pb, h2, :],
                                in_=qfmh[p, h2 * 64:(h2 + 1) * 64, :]).then_inc(p_c.sem, 16)
                    s.dma_start(out=qth[64:128, pb, h2, :],
                                in_=qfmh[p, h2 * 64:(h2 + 1) * 64, :]).then_inc(p_c.sem, 16)
                    s.dma_start(out=qtl[:, pb, h2, :],
                                in_=qfml[p, h2 * 64:(h2 + 1) * 64, :]).then_inc(p_c.sem, 16)
                for kc in range(NKC):
                    for h2 in range(2):
                        s.dma_start(out=vt[:, pb, kc, h2 * 65:h2 * 65 + 64],
                                    in_=vtm[kc * 128:(kc + 1) * 128,
                                            (2 * p + h2) * HD:(2 * p + h2 + 1) * HD]
                                    ).then_inc(p_c.sem, 16)
            p_c.inc(16 * (6 + 2 * NKC))
            kvq_loaded = p_c.v

            for qg in range(NQG):
                evac_prev = v_c.v
                sq_prev = s_c.v

                @block.tensor
                def _(t, p=p, pb=pb, kvq_loaded=kvq_loaded, evac_prev=evac_prev,
                      sq_prev=sq_prev):
                    t.wait_ge(sems[f"p{pb}"].sem, kvq_loaded)
                    t.wait_ge(v_c.sem, max(mcol_done, ones_done, evac_prev))
                    t.wait_ge(s_c.sem, sq_prev)
                    t.nop().then_inc(t_c.sem, 1)
                t_c.inc()

                exp_base = s_c.v
                zb = t_c.v
                def scores_done_val(kc, zb=zb):
                    return zb + 1 if kc == 0 else zb + 2 * kc

                kc_loaded = {}
                prev_sd = sems.get("_prev_scores_done")
                k_base = {0: sems["k0"].v, 1: sems["k1"].v}

                @block.sync
                def _(s, p=p, prev_sd=prev_sd):
                    for kc in range(NKC):
                        k_c = sems[f"k{kc % 2}"]
                        if kc >= 2:
                            s.wait_ge(t_c.sem, scores_done_val(kc - 2))
                        elif prev_sd is not None:
                            s.wait_ge(t_c.sem, prev_sd(NKC - 2 + kc))
                        for h2 in range(2):
                            s.dma_start(out=ktile[0:64, kc % 2, h2, :],
                                        in_=kfmh[p, h2 * 64:(h2 + 1) * 64,
                                                 kc * 128:(kc + 1) * 128]
                                        ).then_inc(k_c.sem, 16)
                            s.dma_start(out=ktile[64:128, kc % 2, h2, :],
                                        in_=kfml[p, h2 * 64:(h2 + 1) * 64,
                                                 kc * 128:(kc + 1) * 128]
                                        ).then_inc(k_c.sem, 16)
                for kc in range(NKC):
                    kc_loaded[kc] = sems[f"k{kc % 2}"].inc(64)

                @block.tensor
                def _(t, p=p, pb=pb, qg=qg, exp_base=exp_base,
                      kc_loaded=dict(kc_loaded)):
                    for kc in range(NKC):
                        kb = kc % 2
                        t.wait_ge(sems[f"k{kb}"].sem, kc_loaded[kc])
                        if kc >= 2:
                            t.wait_ge(s_c.sem, exp_base + 2 * (kc - 1))
                        for h2 in range(2):
                            t.matmul(ps_sc[:, kb, h2, :], ktile[:, kb, h2, :],
                                     qth[:, pb, h2, qg * QGRP:(qg + 1) * QGRP],
                                     start=True, stop=False, skip_group_check=True)
                        t.matmul(ps_sc[:, kb, 0, :], ktile[0:64, kb, 0, :],
                                 qtl[:, pb, 0, qg * QGRP:(qg + 1) * QGRP],
                                 start=False, stop=True, skip_group_check=True)
                        t.matmul(ps_sc[:, kb, 1, :], ktile[0:64, kb, 1, :],
                                 qtl[:, pb, 1, qg * QGRP:(qg + 1) * QGRP],
                                 start=False, stop=True, skip_group_check=True
                                 ).then_inc(t_c.sem, 1)
                        if kc >= 1:
                            t.wait_ge(s_c.sem, exp_base + 2 * kc)
                            for h2 in range(2):
                                last = t.matmul(
                                    ps_po[:, h2, :],
                                    vt[:, pb, kc - 1, h2 * 65:(h2 + 1) * 65],
                                    pt[:, (kc - 1) % 2, h2, :],
                                    start=(kc - 1 == 0), stop=(kc - 1 == NKC - 1))
                            last.then_inc(t_c.sem, 1)
                scores_done = {}
                for kc in range(NKC):
                    scores_done[kc] = t_c.inc()
                    assert scores_done[kc] == scores_done_val(kc)
                    if kc >= 1:
                        t_c.inc()

                @block.scalar
                def _(s, p=p, exp_base=exp_base, zb=zb):
                    for kc in range(NKC):
                        kb = kc % 2
                        s.wait_ge(t_c.sem, scores_done_val(kc))
                        for h2 in range(2):
                            s.activation(out=pt[:, kb, h2, :], in_=ps_sc[:, kb, h2, :],
                                         func=AF.Exp,
                                         bias=mcol[:, 2 * p + h2:2 * p + h2 + 1],
                                         scale=SCALE).then_inc(s_c.sem, 1)
                s_c.inc(2 * NKC)
                sems["_prev_scores_done"] = scores_done_val

                if pending_nsq is not None:
                    emit_nsq(pending_nsq)
                    pending_nsq = None

                @block.tensor
                def _(t, p=p, pb=pb, exp_base=exp_base):
                    t.wait_ge(s_c.sem, exp_base + 2 * NKC)
                    for h2 in range(2):
                        _last = t.matmul(ps_po[:, h2, :],
                                 vt[:, pb, NKC - 1, h2 * 65:(h2 + 1) * 65],
                                 pt[:, (NKC - 1) % 2, h2, :],
                                 start=False, stop=True)
                    _last.then_inc(t_c.sem, 1)
                av_done = t_c.inc()

                @block.vector
                def _(v, p=p, qg=qg, av_done=av_done, ot_prev=ot_store.get(p - 1)):
                    v.wait_ge(t_c.sem, av_done)
                    if ot_prev is not None:
                        v.wait_ge(g_c.sem, ot_prev)
                    for h2 in range(2):
                        v.tensor_copy(out=ot_h[h2][:, qg * QGRP:(qg + 1) * QGRP],
                                      in_=ps_po[0:64, h2, :])
                    for h2 in range(2):
                        _last = v.tensor_copy(out=lsb[64:65, h2, qg * QGRP:(qg + 1) * QGRP],
                                              in_=ps_po[64:65, h2, :])
                    _last.then_inc(v_c.sem, 1)
                oc_done = v_c.inc()

                sqd = []
                for h2 in range(2):
                    @block.scalar
                    def _(s, h2=h2, oc_done=oc_done):
                        s.wait_ge(v_c.sem, oc_done)
                        s.activation(out=sqsb[:, h2, :],
                                     in_=ps_po[0:64, h2, :],
                                     func=AF.Square).then_inc(s_c.sem, 1)
                    sqd.append(s_c.inc())
                pending_nsq = (qg, sqd)
                evac_done_here = s_c.v

            if pending_nsq is not None:
                emit_nsq(pending_nsq)
                pending_nsq = None
            pair_free[p] = t_c.v
            last_ev = v_c.v

            @block.gpsimd
            def _(g, p=p, last_ev=last_ev):
                g.wait_ge(v_c.sem, last_ev)
                g.dma_start(out=otp_out[p, 0:64, :], in_=otA[:, :]).then_inc(g_c.sem, 16)
                g.dma_start(out=otp_out[p, 64:128, :], in_=otB[:, :]).then_inc(g_c.sem, 16)
                for h2 in range(2):
                    g.dma_start(out=l_out[2 * p + h2:2 * p + h2 + 1, :],
                                in_=lsb[64:65, h2, :]).then_inc(g_c.sem, 16)
                    g.dma_start(out=nsq_out[2 * p + h2:2 * p + h2 + 1, :],
                                in_=nsqsb[0:1, h2, :]).then_inc(g_c.sem, 16)
            ot_store[p] = g_c.inc(16 * 6)

        @block.gpsimd
        def _(g):
            g.wait_ge(g_c.sem, g_c.v)
    return nc



def build_l3():
    nc = bass.Bass("TRN2", target_bir_lowering=False, debug=False)
    x1r = nc.dram_tensor("x1r", [D, T], F32, kind="ExternalInput").ap()
    otr = nc.dram_tensor("otr", [NP, 128, T], F32, kind="ExternalInput").ap()
    lg = nc.dram_tensor("lg", [H, TOK], F32, kind="ExternalInput").ap()
    nsqg = nc.dram_tensor("nsqg", [H, TOK], F32, kind="ExternalInput").ap()
    lrot = nc.dram_tensor("lrot", [H, T], F32, kind="ExternalInput").ap()
    wo = nc.dram_tensor("wo", [NP, 128, D], F32, kind="ExternalInput").ap()
    bo = nc.dram_tensor("bo", [H, D], F32, kind="ExternalInput").ap()
    gate = nc.dram_tensor("gate", [1, H], F32, kind="ExternalInput").ap()
    hbm = _declare_ffn_hbm(nc)
    y_out = nc.dram_tensor("y_own", [D, OWN], F32, kind="ExternalOutput").ap()
    x2scr = nc.dram_tensor("x2_scr", [D, OWN], F32, kind="Internal").ap()
    linvscr = nc.dram_tensor("linv_scr", [H, T], F32, kind="Internal").ap()

    es = contextlib.ExitStack()
    with es:
        sems = _mk_sems(nc, es)
        t_c, s_c, v_c, d_c, g_c = (sems[k] for k in ("t", "s", "v", "d", "g"))
        block = es.enter_context(nc.Block())
        names = _load_consts(nc, block, es, sems)

        st = {
            "ps_sum": es.enter_context(nc.psum_tensor("ps_sum", [128, BLK], F32)),
            "ps_sq": es.enter_context(nc.psum_tensor("ps_sq", [128, BLK], F32)),
            "xsq": es.enter_context(nc.sbuf_tensor("xsq", [128, NC_D, BLK], F32)),
            "a_sb": es.enter_context(nc.sbuf_tensor("a_sb", [128, BLK], F32)),
            "b_sb": es.enter_context(nc.sbuf_tensor("b_sb", [128, BLK], F32)),
            "eps": es.enter_context(nc.sbuf_tensor("eps_col", [128, 1], F32)),
            "prev": {},
        }
        gacc = es.enter_context(nc.sbuf_tensor("gacc", [128, NC_D], F32))
        rtmp = es.enter_context(nc.sbuf_tensor("rtmp", [128, NC_D], F32))
        gf = es.enter_context(nc.sbuf_tensor("gf", [1, 1], F32))
        AB_sb = es.enter_context(nc.sbuf_tensor("AB", [1, 2], F32))
        bcol = es.enter_context(nc.sbuf_tensor("bcol", [128, NC_D], F32))

        @block.vector
        def _(v):
            v.memset(gacc[:, :], 0.0)
            v.memset(st["eps"][:, :], EPS).then_inc(v_c.sem, 1)
        v_c.inc()

        with contextlib.ExitStack() as es2:
            gate_sb = es2.enter_context(nc.sbuf_tensor("gate_sb", [1, H], F32))
            den_sb = es2.enter_context(nc.sbuf_tensor("den_sb", [1, H], F32))
            wfac = es2.enter_context(nc.sbuf_tensor("wfac", [1, H], F32))
            wcol = es2.enter_context(nc.sbuf_tensor("wcol", [128, NP], F32))
            gcol16 = es2.enter_context(nc.sbuf_tensor("gcol16", [16, 1], F32))
            bo_sb = es2.enter_context(nc.sbuf_tensor("bo_sb", [16, D], F32))
            wo_sb = es2.enter_context(nc.sbuf_tensor("wo_sb", [128, NP, D], F32))
            ps_a = es2.enter_context(nc.psum_tensor("ps_a", [128, 2, BLK], F32))
            ps_w = es2.enter_context(nc.psum_tensor("ps_w", [128, 16], F32))

            @block.sync
            def _(s):
                s.dma_start(out=gate_sb[0:1, :], in_=gate[0:1, :]).then_inc(d_c.sem, 16)
                s.dma_start(out=bo_sb[:, :], in_=bo[:, :]).then_inc(d_c.sem, 16)
                s.dma_start(out=gcol16[:, 0:1],
                            in_=gate.rearrange("o h -> h o"), allow_slow_non_contiguous=True).then_inc(d_c.sem, 16)
                for p in range(NP):
                    s.dma_start(out=wo_sb[:, p, :], in_=wo[p, :, :]).then_inc(d_c.sem, 16)
            d_c.inc(16 * (3 + NP))
            wo_load = d_c.v

            es3 = contextlib.ExitStack()
            lgt = es3.enter_context(nc.sbuf_tensor("lgt", [128, TOK * H // 128], F32))
            nst = es3.enter_context(nc.sbuf_tensor("nst", [128, TOK * H // 128], F32))
            lrt = es3.enter_context(nc.sbuf_tensor("lrt", [128, T * H // 128], F32))
            nred = es3.enter_context(nc.sbuf_tensor("nred", [128, 1], F32))
            den16 = es3.enter_context(nc.sbuf_tensor("den16", [16, 1], F32))

            k1_c = sems["k1"]

            @block.sync
            def _(s):
                s.dma_start(out=lgt[:, :],
                            in_=lg.rearrange("h (c t) -> (h c) t", c=8)).then_inc(k1_c.sem, 16)
                s.dma_start(out=nst[:, :],
                            in_=nsqg.rearrange("h (c t) -> (h c) t", c=8)).then_inc(k1_c.sem, 16)
                s.dma_start(out=lrt[:, :],
                            in_=lrot.rearrange("h (c t) -> (h c) t", c=8)).then_inc(k1_c.sem, 16)
            k1_c.inc(16 * 3)
            lg_load = k1_c.v

            @block.vector
            def _(v):
                v.wait_ge(sems["k1"].sem, lg_load)
                v.reciprocal(out=lgt[:, :], in_=lgt[:, :])
                v.drain()
                v.reciprocal(out=lrt[:, :], in_=lrt[:, :])
                v.drain().then_inc(v_c.sem, 1)
            linv_done = v_c.inc()

            @block.gpsimd
            def _(g):
                g.wait_ge(v_c.sem, linv_done)
                g.dma_start(out=linvscr.rearrange("h (c t) -> (h c) t", c=8),
                            in_=lrt[:, :]).then_inc(g_c.sem, 16)
            g_c.inc(16)
            linv_stored = g_c.v

            @block.scalar
            def _(s):
                s.wait_ge(sems["k1"].sem, lg_load)
                s.activation(out=nst[:, :], in_=nst[:, :], func=AF.Sqrt
                             ).then_inc(s_c.sem, 1)
            nsq_sqrt = s_c.inc()

            @block.vector
            def _(v):
                v.wait_ge(s_c.sem, nsq_sqrt)
                v.wait_ge(v_c.sem, linv_done)
                v.tensor_mul(out=nst[:, :], in0=nst[:, :], in1=lgt[:, :])
                v.drain()
                v.reduce_sum(out=nred[:, 0:1], in_=nst[:, :], axis=AX.X
                             ).then_inc(v_c.sem, 1)
            nred_done = v_c.inc()

            @block.tensor
            def _(t):
                t.wait_ge(v_c.sem, nred_done)
                t.matmul(ps_w[0:16, 0:1], names["hsel_sb"][:, :], nred[:, 0:1],
                         start=True, stop=True).then_inc(t_c.sem, 1)
            den_mm0 = t_c.inc()

            @block.vector
            def _(v):
                v.wait_ge(t_c.sem, den_mm0)
                v.tensor_copy(out=den16[:, 0:1], in_=ps_w[0:16, 0:1]).then_inc(v_c.sem, 1)
            den16_done = v_c.inc()

            @block.gpsimd
            def _(g):
                g.wait_ge(v_c.sem, den16_done)
                g.dma_start(out=den_sb[0:1, :], in_=den16[:, 0:1]
                            ).then_inc(sems["k0"].sem, 16)
            den_row_done = sems["k0"].inc(16)

            @block.vector
            def _(v):
                v.wait_ge(sems["k0"].sem, den_row_done)
                v.wait_ge(d_c.sem, wo_load)
                v.tensor_scalar(out=den_sb[0:1, :], in0=den_sb[0:1, :],
                                scalar1=1.0 / TOK, scalar2=1e-5, op0=ALU.mult,
                                op1=ALU.max)
                v.drain()
                v.reciprocal(out=den_sb[0:1, :], in_=den_sb[0:1, :])
                v.drain()
                v.tensor_scalar(out=gate_sb[0:1, :], in0=gate_sb[0:1, :], scalar1=0.0,
                                scalar2=1.0, op0=ALU.max, op1=ALU.min)
                v.tensor_scalar(out=gcol16[:, 0:1], in0=gcol16[:, 0:1], scalar1=0.0,
                                scalar2=1.0, op0=ALU.max, op1=ALU.min)
                v.drain()
                v.tensor_mul(out=wfac[0:1, :], in0=gate_sb[0:1, :], in1=den_sb[0:1, :])
                v.drain()
                v.tensor_scalar(out=wfac[0:1, :], in0=wfac[0:1, :], scalar1=1.0 / H,
                                scalar2=None, op0=ALU.mult)
                v.drain()
                v.drain().then_inc(v_c.sem, 1)
            wfac_done = v_c.inc()

            @block.tensor
            def _(t):
                t.wait_ge(v_c.sem, wfac_done)
                t.matmul(ps_w[:, 0:NP], names["rsa_sb"][0:1, :],
                         wfac[0:1, :].rearrange("o (pp hh) -> o hh pp", hh=2)[:, 0, :],
                         start=True, stop=False, skip_group_check=True)
                _last = t.matmul(ps_w[:, 0:NP], names["rsb_sb"][0:1, :],
                         wfac[0:1, :].rearrange("o (pp hh) -> o hh pp", hh=2)[:, 1, :],
                         start=False, stop=True, skip_group_check=True)
                _last.then_inc(t_c.sem, 1)
            wc_mm = t_c.inc()

            @block.vector
            def _(v):
                v.wait_ge(t_c.sem, wc_mm)
                v.tensor_copy(out=wcol[:, :], in_=ps_w[:, 0:NP])
                v.drain()
                for p in range(NP):
                    _last = v.tensor_scalar(out=wo_sb[:, p, :], in0=wo_sb[:, p, :],
                                    scalar1=wcol[:, p:p + 1], scalar2=None,
                                    op0=ALU.mult)
                _last.then_inc(v_c.sem, 1)
            wo_scaled = v_c.inc()

            @block.vector
            def _(v):
                v.wait_ge(v_c.sem, wfac_done)
                _last = v.tensor_scalar(out=bo_sb[:, :], in0=bo_sb[:, :],
                                scalar1=gcol16[:, 0:1], scalar2=None, op0=ALU.mult)
                _last.then_inc(v_c.sem, 1)
            bo_scaled = v_c.inc()

            @block.tensor
            def _(t):
                t.wait_ge(v_c.sem, bo_scaled)
                for c in range(NC_D):
                    _last = t.matmul(ps_w[0:128, 8 + c:8 + c + 1] if False else ps_a[:, 1, c:c + 1],
                             bo_sb[:, c * 128:(c + 1) * 128],
                             names["ones128_sb"][0:16, 0:1],
                             start=(c == 0), stop=(c == NC_D - 1))
                _last.then_inc(t_c.sem, 1)
            bc_mm = t_c.inc()

            @block.vector
            def _(v):
                v.wait_ge(t_c.sem, bc_mm)
                _last = v.tensor_scalar(out=bcol[:, :], in0=ps_a[:, 1, 0:NC_D], scalar1=1.0 / H,
                                scalar2=None, op0=ALU.mult)
                _last.then_inc(v_c.sem, 1)
            bcol_done = v_c.inc()

            phA_fence = (t_c.v, s_c.v, v_c.v)
            es3.close()
            xblk = es2.enter_context(nc.sbuf_tensor("xblk", [128, 2, NC_D, BLK], F32))
            oblk = es2.enter_context(nc.sbuf_tensor("oblk", [128, 2, NP, BLK], F32))
            x2blk = es2.enter_context(nc.sbuf_tensor("x2blk", [128, NC_D, BLK], F32))
            lnt = es2.enter_context(nc.sbuf_tensor("lnt", [128, NC_D, BLK], F32))
            l2sb = es2.enter_context(nc.sbuf_tensor("l2sb", [2, 2, NP, BLK], F32))
            ps_lb = es2.enter_context(nc.psum_tensor("ps_lb", [128, 2, BLK], F32))
            b_free = {}
            for tb in range(NBLK_B):
                buf = tb % 2

                x_c = sems[f"x{buf}"]

                @block.sync
                def _(s, tb=tb, buf=buf):
                    s.wait_ge(g_c.sem, linv_stored)
                    if tb >= 2:
                        s.wait_ge(v_c.sem, b_free[tb - 2])
                    else:
                        s.wait_ge(t_c.sem, phA_fence[0])
                        s.wait_ge(s_c.sem, phA_fence[1])
                        s.wait_ge(v_c.sem, phA_fence[2])
                    for c in range(NC_D):
                        s.dma_start(out=xblk[:, buf, c, :],
                                    in_=x1r[c * 128:(c + 1) * 128, tb * BLK:(tb + 1) * BLK]
                                    ).then_inc(x_c.sem, 16)
                    for p in range(NP):
                        s.dma_start(out=oblk[:, buf, p, :],
                                    in_=otr[p, :, tb * BLK:(tb + 1) * BLK]
                                    ).then_inc(x_c.sem, 16)
                        s.dma_start(out=l2sb[:, buf, p, :],
                                    in_=linvscr[2 * p:2 * p + 2, tb * BLK:(tb + 1) * BLK]
                                    ).then_inc(x_c.sem, 16)
                x_c.inc(16 * (NC_D + 2 * NP))
                blk_loaded = x_c.v

                div_done = {}
                for p in range(NP):
                    prev_div = div_done.get(p - 2, 0)

                    @block.tensor
                    def _(t, p=p, buf=buf, blk_loaded=blk_loaded, prev_div=prev_div):
                        t.wait_ge(sems[f"x{buf}"].sem, blk_loaded)
                        if p >= 2:
                            t.wait_ge(v_c.sem, prev_div)
                        t.matmul(ps_lb[:, p % 2, :], names["rsab_sb"][:, :],
                                 l2sb[:, buf, p, :], start=True, stop=True
                                 ).then_inc(t_c.sem, 1)
                    lb_mm = t_c.inc()

                    @block.vector
                    def _(v, p=p, buf=buf, lb_mm=lb_mm):
                        v.wait_ge(t_c.sem, lb_mm)
                        v.tensor_mul(out=oblk[:, buf, p, :], in0=oblk[:, buf, p, :],
                                     in1=ps_lb[:, p % 2, :]).then_inc(v_c.sem, 1)
                    div_done[p] = v_c.inc()
                all_div = div_done[NP - 1]

                pj_evac = {}
                scr_g_prev = g_c.v
                for c in range(NC_D):
                    @block.tensor
                    def _(t, c=c, buf=buf, blk_loaded=blk_loaded, pj_evac=pj_evac,
                          tb=tb, scr_g_prev=scr_g_prev, all_div=all_div):
                        t.wait_ge(sems[f"x{buf}"].sem, blk_loaded)
                        t.wait_ge(v_c.sem, max(wo_scaled, bcol_done, all_div))
                        if c >= 2:
                            t.wait_ge(v_c.sem, pj_evac[c - 2])
                        elif tb > 0:
                            t.wait_ge(v_c.sem, b_free[tb - 1])
                            t.wait_ge(g_c.sem, scr_g_prev)
                        for p in range(NP):
                            _last = t.matmul(ps_a[:, c % 2, :], wo_sb[:, p, c * 128:(c + 1) * 128],
                                     oblk[:, buf, p, :], start=(p == 0),
                                     stop=(p == NP - 1))
                        _last.then_inc(t_c.sem, 1)
                    pj_mm = t_c.inc()

                    @block.scalar
                    def _(s, c=c, pj_mm=pj_mm):
                        s.wait_ge(t_c.sem, pj_mm)
                        s.activation(out=x2blk[:, c, :], in_=ps_a[:, c % 2, :],
                                     func=AF.Identity, bias=bcol[:, c:c + 1]
                                     ).then_inc(s_c.sem, 1)
                    pj_act = s_c.inc()

                    @block.vector
                    def _(v, c=c, buf=buf, pj_act=pj_act):
                        v.wait_ge(s_c.sem, pj_act)
                        _last = v.tensor_add(out=x2blk[:, c, :], in0=x2blk[:, c, :],
                                     in1=xblk[:, buf, c, :])
                        _last.then_inc(v_c.sem, 1)
                    pj_evac[c] = v_c.inc()
                x2_done = v_c.v

                if tb < NBLK_O:
                    @block.gpsimd
                    def _(g, tb=tb, x2_done=x2_done):
                        g.wait_ge(v_c.sem, x2_done)
                        for c in range(NC_D):
                            g.dma_start(out=x2scr[c * 128:(c + 1) * 128,
                                                  tb * BLK:(tb + 1) * BLK],
                                        in_=x2blk[:, c, :]).then_inc(g_c.sem, 16)
                    g_c.inc(16 * NC_D)

                ab_ready = _emit_stats(nc, block, es, names, x2blk, st, sems,
                                       v_c.sem, x2_done)
                ln_done = _emit_ln_apply(block, x2blk, lnt, st["a_sb"], st["b_sb"],
                                         sems, ab_ready)

                @block.vector
                def _(v, ln_done=ln_done):
                    v.wait_ge(v_c.sem, ln_done)
                    for c in range(NC_D):
                        v.reduce_sum(out=rtmp[:, c:c + 1], in_=lnt[:, c, :], axis=AX.X)
                    v.drain()
                    v.tensor_add(out=gacc[:, :], in0=gacc[:, :], in1=rtmp[:, :]).then_inc(v_c.sem, 1)
                b_free[tb] = v_c.inc()
            scr_done_g = g_c.v

        (w_sb, b_sbw, gw_sb, gb_sb, arsb, wload) = _ffn_weight_loads(
            nc, block, es, hbm, sems,
            fence=(t_c.v, s_c.v, v_c.v, g_c.v))
        psg = es.enter_context(nc.psum_tensor("ps_gc", [128, 4], F32))
        gf_done = _emit_gcnet(nc, block, es, names, gacc, gw_sb, gb_sb, gf, psg,
                              sems, b_free[NBLK_B - 1], float(T), "3", wload=wload)
        _emit_ab_scalars(block, gf, arsb, AB_sb, sems, gf_done, wload)
        fence3 = (t_c.v, s_c.v, v_c.v)

        gfin = _emit_ffn_block_sweep(nc, block, es, names, w_sb, b_sbw, AB_sb, st,
                                     sems, x2scr, y_out, 0.5,
                                     (g_c.sem, scr_done_g), "3", fence=fence3,
                                     wload=wload)

        @block.gpsimd
        def _(g):
            g.wait_ge(g_c.sem, gfin)
    return nc



_CACHE = {}


def _get(name, builder):
    if name not in _CACHE:
        _CACHE[name] = builder()
    return _CACHE[name]


def _prep_ffn(inp):
    f = np.float32
    w = {}
    for k in FFN_KEYS:
        src = "L0_ffn_adaptive_range" if k == "ar" else f"L0_ffn_{k}"
        w[k] = np.ascontiguousarray(np.asarray(inp[src], f))
    return w


def run_l1(inp):
    f = np.float32
    x0 = np.asarray(inp["L0_attn_hidden_states"], dtype=f)
    x0fm = [np.ascontiguousarray(x0[b].T) for b in range(B)]
    ffn_w = _prep_ffn(inp)
    nc1 = _get("l1", build_l1)
    in_maps = []
    for c in range(8):
        b, roff = c // 4, (c % 4) * OWN
        m = {"x_all": x0fm[b],
             "x_own": np.ascontiguousarray(x0fm[b][:, roff:roff + OWN])}
        m.update(ffn_w)
        in_maps.append(m)
    r1 = run_bass_kernel_spmd(nc1, in_maps, core_ids=list(range(8)))
    x1_own = [r1.results[c]["x1_own"] for c in range(8)]
    return [np.ascontiguousarray(np.concatenate(x1_own[4 * b:4 * b + 4], axis=1))
            for b in range(B)]


def run_l2(inp, x1fm):
    f = np.float32
    wq_all = np.ascontiguousarray(
        np.asarray(inp["L0_attn_Wq"], f).transpose(1, 0, 2).reshape(D, D))
    wk_all = np.ascontiguousarray(
        np.asarray(inp["L0_attn_Wk"], f).transpose(1, 0, 2).reshape(D, D))
    wv_all = np.ascontiguousarray(
        np.asarray(inp["L0_attn_Wv"], f).transpose(1, 0, 2).reshape(D, D))
    bq_all = np.ascontiguousarray(np.asarray(inp["L0_attn_bq"], f).reshape(D))
    bk_all = np.ascontiguousarray(np.asarray(inp["L0_attn_bk"], f).reshape(D))
    bv_all = np.ascontiguousarray(np.asarray(inp["L0_attn_bv"], f).reshape(D))
    nc2 = _get("l2", build_l2)
    in_maps = []
    for c in range(8):
        b, roff = c // 4, (c % 4) * OWN
        in_maps.append({
            "x_all": x1fm[b],
            "x_own": np.ascontiguousarray(x1fm[b][:, roff:roff + OWN]),
            "wq": wq_all, "wk": wk_all, "wv": wv_all,
            "bq": bq_all, "bk": bk_all, "bv": bv_all,
        })
    r2 = run_bass_kernel_spmd(nc2, in_maps, core_ids=list(range(8)))
    otp = [r2.results[c]["otp"] for c in range(8)]
    ot_full = [np.ascontiguousarray(np.concatenate(otp[4 * b:4 * b + 4], axis=2))
               for b in range(B)]
    l_byb = [np.ascontiguousarray(np.concatenate(
        [r2.results[c]["l_out"] for c in range(4 * b, 4 * b + 4)], axis=1))
        for b in range(B)]
    nsq_byb = [np.ascontiguousarray(np.concatenate(
        [r2.results[c]["nsq_out"] for c in range(4 * b, 4 * b + 4)], axis=1))
        for b in range(B)]
    return ot_full, l_byb, nsq_byb


def run_l3(inp, x1fm, ot_full, l_byb, nsq_byb):
    f = np.float32
    wo_p = np.ascontiguousarray(np.asarray(inp["L0_attn_Wo"], f).reshape(NP, 128, D))
    bo_in = np.ascontiguousarray(np.asarray(inp["L0_attn_bo"], f))
    gate_in = np.ascontiguousarray(np.asarray(inp["L0_attn_gate"], f).reshape(1, H))
    ffn_w = _prep_ffn(inp)
    nc3 = _get("l3", build_l3)
    lg = np.ascontiguousarray(np.concatenate(l_byb, axis=1))
    nsqg = np.ascontiguousarray(np.concatenate(nsq_byb, axis=1))
    in_maps = []
    for c in range(8):
        b, roff = c // 4, (c % 4) * OWN
        m = {"x1r": np.ascontiguousarray(np.roll(x1fm[b], -roff, axis=1)),
             "otr": np.ascontiguousarray(np.roll(ot_full[b], -roff, axis=2)),
             "lg": lg, "nsqg": nsqg,
             "lrot": np.ascontiguousarray(np.roll(l_byb[b], -roff, axis=1)),
             "wo": wo_p, "bo": bo_in, "gate": gate_in}
        m.update(ffn_w)
        in_maps.append(m)
    r3 = run_bass_kernel_spmd(nc3, in_maps, core_ids=list(range(8)))
    return [r3.results[c]["y_own"] for c in range(8)]


def kernel(**inp):
    x1fm = run_l1(inp)
    ot_full, l_byb, nsq_byb = run_l2(inp, x1fm)
    y_own = run_l3(inp, x1fm, ot_full, l_byb, nsq_byb)
    y = np.empty((B, T, D), dtype=np.float32)
    for c in range(8):
        b, roff = c // 4, (c % 4) * OWN
        y[b, roff:roff + OWN, :] = y_own[c].T
    return y
